# version 03
# baseline (speedup 1.0000x reference)
import numpy as np
import ml_dtypes
import concourse.bass as bass
import concourse.mybir as mybir
from concourse.bass_utils import run_bass_kernel_spmd

F32 = mybir.dt.float32
BF16 = mybir.dt.bfloat16
U8 = mybir.dt.uint8
AF = mybir.ActivationFunctionType
AX = mybir.AxisListType
ALU = mybir.AluOpType

B, S, H, NH, HD, FF = 4, 4096, 2048, 16, 128, 4096
T = S // 2
KC, OC, FC = H // 128, H // 128, FF // 128
NT = S // 128
NQ = T // 128
NEGSTAB = -4.0
EPS = 1e-5


class SemT:
    def __init__(self, nc, name):
        self.h = nc.alloc_semaphore(name=name)
        self.n = 0

    def tag(self, k=1):
        self.n += k
        return self.n


def build(upto="full"):
    nc = bass.Bass("TRN2", target_bir_lowering=False, debug=False)
    dbg = upto != "full"

    x_d = nc.dram_tensor("x", [S, H], F32, kind="ExternalInput").ap()
    w_d = {nm: nc.dram_tensor(nm, shp, F32, kind="ExternalInput").ap()
           for nm, shp in [("wqt", [H, H]), ("wkt", [H, H]), ("wvt", [H, H]),
                           ("wot", [H, H]), ("w0at", [H, FF]), ("w0bt", [FF, H]),
                           ("w1at", [H, FF]), ("w1bt", [FF, H])]}
    b_d = {nm: nc.dram_tensor(nm, [n], F32, kind="ExternalInput").ap()
           for nm, n in [("bq", H), ("bk", H), ("bv", H), ("b0a", FF),
                         ("b0b", H), ("b1a", FF), ("b1b", H)]}
    wgt_d = nc.dram_tensor("wgt", [H, 2], F32, kind="ExternalInput").ap()
    cosq_d = nc.dram_tensor("cosq", [128, T], BF16, kind="ExternalInput").ap()
    sinq_d = nc.dram_tensor("sinq", [128, T], BF16, kind="ExternalInput").ap()
    cosk_d = nc.dram_tensor("cosk", [128, S], BF16, kind="ExternalInput").ap()
    sink_d = nc.dram_tensor("sink", [128, S], BF16, kind="ExternalInput").ap()
    ident_d = nc.dram_tensor("ident", [128, 128], BF16, kind="ExternalInput").ap()
    ones128_d = nc.dram_tensor("ones128", [128, 128], F32, kind="ExternalInput").ap()
    onesrow_d = nc.dram_tensor("onesrow", [1, 128], BF16, kind="ExternalInput").ap()
    out_d = nc.dram_tensor("out", [T, H], F32, kind="ExternalOutput").ap()

    def scr(name, shape, dtype, expose):
        kind = "ExternalOutput" if (dbg and expose) else "Internal"
        return nc.dram_tensor(name, shape, dtype, kind=kind).ap()

    qT_s = scr("qT_s", [NH, 128, T], BF16, upto in ("qkv", "c1", "c2", "c4", "fullx"))
    kT_s = scr("kT_s", [NH, 128, S], BF16, upto in ("qkv", "c1", "c2", "c4", "fullx"))
    vT_s = scr("vT_s", [NH, 128, S], BF16, upto in ("qkv", "c1", "c2", "c4", "fullx"))
    attn_s = scr("attn_s", [NH, 128, T], BF16, upto in ("attn", "c1", "c2", "c4", "fullx"))
    x2_s = scr("x2_s", [NQ, 128, H], F32, upto in ("wo", "fullx"))
    ln2T_s = scr("ln2T_s", [KC, 128, T], BF16, upto in ("wo", "fullx"))
    ln1_dbg = scr("ln1_dbg", [KC, 128, S], BF16, upto in ("ln1", "a2"))
    stats_dbg = scr("stats_dbg", [128, 8 * 64], F32, upto == "a2")

    A = nc.alloc_sbuf_tensor
    bt = {nm: A(nm + "_t", [128, OC], F32) for nm in ["bq", "bk", "bv", "b0b", "b1b"]}
    bt["b0a"] = A("b0a_t", [128, FC], F32)
    bt["b1a"] = A("b1a_t", [128, FC], F32)
    bqsw = A("bqsw", [128, OC], F32)
    bksw = A("bksw", [128, OC], F32)
    wg_t = A("wg_t", [128, 2 * KC], F32)
    wgd_t = A("wgd_t", [128, KC], BF16)
    ident_t = A("ident_t", [128, 128], BF16)
    ones128_t = A("ones128_t", [128, 128], F32)
    onesrow_t = A("onesrow_t", [1, 128], BF16)
    consts_t = A("consts_t", [128, 4], F32)
    stats_t = A("stats_t", [128, 8 * 64], F32)
    xbuf = [A(f"xbuf{i}", [128, H], F32) for i in range(2)]
    wblk = [A(f"wblk{i}", [128, FF], BF16) for i in range(4)]
    spill = [A(f"spill{i}", [128, T], BF16) for i in range(2)]
    lnb = [A(f"lnb{i}", [128, H], BF16) for i in range(2)]
    sqtmp = A("sqtmp", [128, H], BF16)

    ST_XSUM, ST_SSUM, ST_MEAN, ST_EX2, ST_VAR, ST_RSTD, ST_NMR, ST_TMP = range(8)

    def stc(kind, tile):
        return stats_t[:, kind * 64 + tile: kind * 64 + tile + 1]

    def str_(kind, t0, n):
        return stats_t[:, kind * 64 + t0: kind * 64 + t0 + n]

    SN = ["set", "setc",
          "x", "sq", "vs", "st1", "st2", "st3", "ln", "tp", "tpf",
          "w", "qkvp", "evr", "evv", "sp", "rt", "rtc",
          "kvq", "tpv", "tpvf", "sc", "exp", "vm", "dng", "denb", "den",
          "oev", "aev", "hd",
          "wo", "attnl", "y", "xr", "x2", "sq2", "vs2", "st21", "st22", "st23",
          "x2r", "ln2", "tp2", "tpf2", "spl2", "spx2",
          "lnq", "gate", "msk", "mskp", "mskf", "w2", "hp", "hev", "op", "oevB",
          "s1", "bl", "tpe", "xe", "fin", "outd", "dbg"]
    sm = {nm: SemT(nc, "s_" + nm) for nm in SN}

    a1 = []
    for tt in range(NT):
        a1.append(dict(tt=tt, t_x=sm["x"].tag(16), t_sq=sm["sq"].tag(),
                       t_vs=sm["vs"].tag()))
    T_SQ_FENCE = sm["sq"].tag()
    _ = [sm["vs"].tag() for _ in range(8)]
    a2 = []
    for h2 in range(2):
        for lt in range(16):
            e = dict(h2=h2, lt=lt, tt=h2 * 16 + lt, gi2=len(a2))
            e["t_x"] = sm["x"].tag(16)
            e["t_ln"] = sm["ln"].tag()
            e["t_tp"] = [sm["tp"].tag() for _ in range(4)]
            e["t_tpf"] = [sm["tpf"].tag() for _ in range(4)]
            a2.append(e)

    b_sweeps = []
    for h2 in range(2):
        wis = [0, 1, 2] if h2 == 0 else [1, 2]
        for wi in wis:
            for oc in range(OC):
                b_sweeps.append(dict(h2=h2, wi=wi, oc=oc))
    if upto == "bq":
        b_sweeps = [z for z in b_sweeps if z["wi"] == 0]
    elif upto in ("bq1", "bqm"):
        b_sweeps = [z for z in b_sweeps if z["wi"] == 0 and z["oc"] == 0]
    elif upto == "bqk":
        b_sweeps = [z for z in b_sweeps if z["wi"] != 2]
    elif upto == "bv":
        b_sweeps = [z for z in b_sweeps if z["wi"] == 2]
    for g, ent in enumerate(b_sweeps):
        ent["g"] = g
        ent["t_w"] = sm["w"].tag(16)
        ent["t_qkvp"] = sm["qkvp"].tag()
        if ent["wi"] in (0, 1):
            ent["ev"] = ("evr", sm["evr"].tag())
            ent["rt"] = [sm["rt"].tag(2) for _ in range(4)]
            ent["rtc"] = [sm["rtc"].tag(2) for _ in range(4)]
        else:
            ent["ev"] = ("evv", sm["evv"].tag())
        ent["t_sp"] = sm["sp"].tag(16)
    NB = len(b_sweeps)

    NHC = {"c1": 1, "c2": 2, "c4": 4}.get(upto, NH)
    c_hq = []
    for h in range(NHC):
        for qb in range(2):
            q = dict(h=h, qb=qb, hq=h * 2 + qb)
            if qb == 0:
                q["t_kvq"] = sm["kvq"].tag(48)
                q["t_tpv"] = [sm["tpv"].tag() for _ in range(8)]
                q["t_tpvf"] = [sm["tpvf"].tag() for _ in range(8)]
            q["t_sc"] = [sm["sc"].tag() for _ in range(32)]
            q["t_exp"] = [sm["exp"].tag() for _ in range(32)]
            q["t_vm"] = [sm["vm"].tag() for _ in range(32)]
            q["t_dng"] = [sm["dng"].tag(4) for _ in range(8)]
            _ = sm["dng"].tag(2)
            q["t_dncmb"] = sm["dng"].tag()
            q["t_denb"] = sm["denb"].tag()
            _ = sm["den"].tag()
            q["t_den"] = sm["den"].tag()
            q["t_oev"] = sm["oev"].tag()
            q["t_aev"] = sm["aev"].tag(16)
            if qb == 1:
                q["t_hd"] = sm["hd"].tag()
            c_hq.append(q)

    def chq(hq):
        return c_hq[hq]

    t_wo = [sm["wo"].tag(16) for _ in range(OC)]
    d_qc = []
    for qc in range(NQ):
        e = dict(qc=qc, t_attnl=sm["attnl"].tag(16), t_y=sm["y"].tag(),
                 t_xr=sm["xr"].tag(16), t_x2=sm["x2"].tag(),
                 t_sq2=sm["sq2"].tag(), t_vs2=sm["vs2"].tag(),
                 t_spx2=sm["spx2"].tag(16))
        d_qc.append(e)
    T_SQ2_FENCE = sm["sq2"].tag()
    _ = [sm["vs2"].tag() for _ in range(8)]
    d2_qc = []
    for qc in range(NQ):
        e = dict(qc=qc, t_x2r=sm["x2r"].tag(16), t_ln2=sm["ln2"].tag(),
                 t_tp2=[sm["tp2"].tag() for _ in range(4)],
                 t_tpf2=[sm["tpf2"].tag() for _ in range(4)],
                 t_spl2=sm["spl2"].tag(16))
        d2_qc.append(e)

    e_blocks = []
    e_qq = []
    for qq in range(4):
        e = dict(qq=qq)
        e["t_lnq"] = sm["lnq"].tag(16 * 16)
        e["t_gate"] = sm["gate"].tag()
        e["t_mskcp"] = sm["msk"].tag()
        e["t_msk"] = sm["msk"].tag()
        e["t_mskp"] = sm["mskp"].tag()
        e["t_mskf"] = sm["mskf"].tag()
        e["ph"] = []
        for pi, nblk in enumerate([FC, OC, FC, OC]):
            p = dict(pi=pi, blk=[])
            for j in range(nblk):
                bb = dict(qq=qq, pi=pi, j=j)
                bb["t_w2"] = sm["w2"].tag(16)
                if pi in (0, 2):
                    bb["t_hp"] = sm["hp"].tag()
                    bb["t_hev"] = sm["hev"].tag()
                else:
                    bb["t_op"] = sm["op"].tag()
                    if pi == 1:
                        bb["t_oevB"] = sm["oevB"].tag()
                    else:
                        bb["t_s1"] = sm["s1"].tag()
                        _ = sm["bl"].tag()
                        bb["t_bl"] = sm["bl"].tag()
                bb["stream_i"] = len(e_blocks)
                e_blocks.append(bb)
                p["blk"].append(bb)
            e["ph"].append(p)
        e["fin"] = []
        for qcB in range(4):
            f = dict(qcB=qcB, gqc=qq * 4 + qcB)
            f["t_tpe"] = sm["tpe"].tag()
            f["t_xe"] = sm["xe"].tag(16)
            f["t_fin"] = sm["fin"].tag()
            f["t_out"] = sm["outd"].tag(16)
            e["fin"].append(f)
        e_qq.append(e)

    with nc.Block("setup") as blk:
        @blk.sync
        def _(e):
            for dst, src in [(ident_t, ident_d), (ones128_t, ones128_d),
                             (onesrow_t, onesrow_d)]:
                e.dma_start(out=dst[:], in_=src[:]).then_inc(sm["set"].h, 16)

        @blk.gpsimd
        def _(e):
            e.memset(consts_t[:, 0:1], NEGSTAB).then_inc(sm["setc"].h, 1)
            for nm in ["bq", "bk", "bv", "b0b", "b1b", "b0a", "b1a"]:
                e.dma_start(out=bt[nm][:], in_=b_d[nm].rearrange("(o p) -> p o", p=128),
                            allow_slow_non_contiguous=True).then_inc(sm["set"].h, 16)
            e.dma_start(out=wg_t[:].rearrange("p (k e) -> p k e", e=2),
                        in_=wgt_d.rearrange("(k p) e -> p k e", p=128),
                        allow_slow_non_contiguous=True).then_inc(sm["set"].h, 16)

        @blk.vector
        def _(e):
            e.wait_ge(sm["set"].h, 16 * 11)
            e.tensor_scalar_mul(bqsw[0:64, :], bt["bq"][64:128, :], -1.0)
            e.tensor_scalar_mul(bqsw[64:128, :], bt["bq"][0:64, :], 1.0)
            e.tensor_scalar_mul(bksw[0:64, :], bt["bk"][64:128, :], -1.0)
            e.tensor_scalar_mul(bksw[64:128, :], bt["bk"][0:64, :], 1.0)
            wg3 = wg_t[:].rearrange("p (k e) -> p k e", e=2)
            e.tensor_tensor(wgd_t[:], wg3[:, :, 0], wg3[:, :, 1],
                            op=ALU.subtract).then_inc(sm["setc"].h, 1)

    sm["setc"].n = 2
    SET_AB = 16 * 15

    if upto == "setup":
        with nc.Block("term") as blk:
            @blk.sync
            def _(e):
                e.wait_ge(sm["set"].h, 16 * 11)
                e.wait_ge(sm["setc"].h, 2)
        return nc

    sbA = nc.sbuf_base
    ln1T = nc.alloc_sbuf_tensor("ln1T", [128, KC * T], BF16)
    cosq_t = nc.alloc_sbuf_tensor("cosq_t", [128, T], BF16)
    sinq_t = nc.alloc_sbuf_tensor("sinq_t", [128, T], BF16)
    cosk_t = nc.alloc_sbuf_tensor("cosk_t", [128, S], BF16)
    sink_t = nc.alloc_sbuf_tensor("sink_t", [128, S], BF16)
    ropet = [nc.alloc_sbuf_tensor(f"ropet{i}", [128, 512], F32) for i in range(2)]
    ropeu = [nc.alloc_sbuf_tensor(f"ropeu{i}", [128, 512], F32) for i in range(2)]
    ropea = nc.alloc_sbuf_tensor("ropea", [128, 512], F32)
    ropeb = nc.alloc_sbuf_tensor("ropeb", [128, 512], F32)

    psA = nc.psum_base
    pg = [nc.alloc_psum_tensor(f"pg{i}", [128, 512], F32) for i in range(8)]
    nc.psum_base = psA
    tp_ps = [nc.alloc_psum_tensor(f"tp{i}", [128, 512], BF16) for i in range(2)]
    nc.psum_base = psA + 8

    def ln1c(kc, c0, n):
        return ln1T[:, kc * T + c0: kc * T + c0 + n]

    with nc.Block("AB") as blk:
        @blk.sync
        def _(e):
            for dst, src in [(cosq_t, cosq_d), (sinq_t, sinq_d),
                             (cosk_t, cosk_d), (sink_t, sink_d)]:
                e.dma_start(out=dst[:], in_=src[:]).then_inc(sm["set"].h, 16)
            for ent in a1:
                tt = ent["tt"]
                if tt >= 2:
                    e.wait_ge(sm["sq"].h, a1[tt - 2]["t_sq"])
                    e.wait_ge(sm["vs"].h, a1[tt - 2]["t_vs"])
                e.dma_start(out=xbuf[tt % 2][:],
                            in_=x_d[tt * 128:(tt + 1) * 128, :]).then_inc(sm["x"].h, 16)
            if upto == "a1":
                e.wait_ge(sm["st3"].h, 1)
                e.wait_ge(sm["st2"].h, 1)
                return
            for h2 in range(2):
                for ent in [z for z in a2 if z["h2"] == h2]:
                    tt, gi2 = ent["tt"], ent["gi2"]
                    if gi2 >= 2:
                        e.wait_ge(sm["ln"].h, a2[gi2 - 2]["t_ln"])
                    else:
                        e.wait_ge(sm["sq"].h, a1[NT - 1]["t_sq"])
                        e.wait_ge(sm["vs"].h, a1[NT - 1]["t_vs"])
                    e.dma_start(out=xbuf[gi2 % 2][:],
                                in_=x_d[tt * 128:(tt + 1) * 128, :]).then_inc(sm["x"].h, 16)
                for ent in ([] if upto == "a2" else [z for z in b_sweeps if z["h2"] == h2]):
                    sem, tgt = ent["ev"]
                    e.wait_ge(sm[sem].h, tgt)
                    wi, oc = ent["wi"], ent["oc"]
                    dest = [qT_s, kT_s, vT_s][wi]
                    col0 = 0 if wi == 0 else h2 * T
                    e.dma_start(out=dest[oc, :, col0:col0 + T],
                                in_=spill[ent["g"] % 2][:]).then_inc(sm["sp"].h, 16)
            if upto in ("qkv", "bq", "bq1", "bqm", "bqk", "bv"):
                e.wait_ge(sm["sp"].h, sm["sp"].n)
            if upto in ("ln1", "a2"):
                e.wait_ge(sm["tpf"].h, a2[-1]["t_tpf"][3])
                if upto == "a2":
                    e.dma_start(out=stats_dbg[:, :], in_=stats_t[:]).then_inc(sm["dbg"].h, 16)
                for kc in range(KC):
                    e.dma_start(out=ln1_dbg[kc, :, T:S], in_=ln1c(kc, 0, T)
                                ).then_inc(sm["dbg"].h, 16)
                e.wait_ge(sm["dbg"].h, 16 * KC)

        @blk.gpsimd
        def _(e):
            if upto in ("a1", "a2"):
                return
            wts = {0: w_d["wqt"], 1: w_d["wkt"], 2: w_d["wvt"]}
            for ent in b_sweeps:
                g, wi, oc = ent["g"], ent["wi"], ent["oc"]
                if g >= 4:
                    e.wait_ge(sm["qkvp"].h, b_sweeps[g - 4]["t_qkvp"])
                e.dma_start(
                    out=wblk[g % 4][:, 0:KC * 128].rearrange("p (k c) -> p k c", k=KC),
                    in_=wts[wi][:, oc * 128:(oc + 1) * 128].rearrange(
                        "(k p) c -> p k c", p=128),
                ).then_inc(sm["w"].h, 16)

        @blk.scalar
        def _(e):
            for ent in a1:
                tt = ent["tt"]
                e.wait_ge(sm["x"].h, ent["t_x"])
                e.activation(sqtmp[:], xbuf[tt % 2][:], AF.Square,
                             accum_out=stc(ST_SSUM, tt)).then_inc(sm["sq"].h, 1)
            e.copy(str_(ST_TMP, 32, NT), str_(ST_SSUM, 0, NT)).then_inc(sm["sq"].h, 1)
            e.wait_ge(sm["st1"].h, 1)
            e.activation(str_(ST_RSTD, 0, NT), str_(ST_VAR, 0, NT),
                         AF.Sqrt).then_inc(sm["st2"].h, 1)
            if upto == "a1":
                return
            e.wait_ge(sm["st3"].h, 1)
            e.wait_ge(sm["setc"].h, 2)
            for h2 in range(2):
                for ent in [z for z in a2 if z["h2"] == h2]:
                    tt, gi2 = ent["tt"], ent["gi2"]
                    e.wait_ge(sm["x"].h, ent["t_x"])
                    if gi2 >= 2:
                        e.wait_ge(sm["tp"].h, a2[gi2 - 2]["t_tp"][3])
                    e.activation(lnb[gi2 % 2][:], xbuf[gi2 % 2][:], AF.Identity,
                                 bias=stc(ST_NMR, tt),
                                 scale=stc(ST_RSTD, tt)).then_inc(sm["ln"].h, 1)
                for ent in ([] if upto == "a2" else [z for z in b_sweeps if z["h2"] == h2]):
                    wi, oc, g = ent["wi"], ent["oc"], ent["g"]
                    e.wait_ge(sm["qkvp"].h, ent["t_qkvp"])
                    if g >= 2:
                        e.wait_ge(sm["sp"].h, b_sweeps[g - 2]["t_sp"])
                    if wi == 2:
                        for ti in range(4):
                            ins = e.activation(spill[g % 2][:, ti * 512:(ti + 1) * 512],
                                               pg[(g % 2) * 4 + ti][:], AF.Identity,
                                               bias=bt["bv"][:, oc:oc + 1])
                            if ti == 3:
                                ins.then_inc(sm["evv"].h, 1)
                    elif upto != "bqm":
                        bias_t = bt["bq"] if wi == 0 else bt["bk"]
                        for ti in range(4):
                            ro = ent["rt"][ti]
                            if ro > 4:
                                e.wait_ge(sm["rtc"].h, ro - 4)
                            bank = pg[(g % 2) * 4 + ti]
                            ub = ropeu[(ro // 2) % 2]
                            buf = ropet[(ro // 2) % 2]
                            e.activation(ub[:], bank[:], AF.Identity,
                                         bias=bias_t[:, oc:oc + 1]).then_inc(sm["rt"].h, 1)
                            e.wait_ge(sm["rt"].h, ro - 1)
                            e.activation(buf[0:64, :], ub[64:128, :], AF.Identity,
                                         scale=-1.0)
                            e.activation(buf[64:128, :], ub[0:64, :],
                                         AF.Identity).then_inc(sm["rt"].h, 1)

        @blk.vector
        def _(e):
            for ent in a1:
                tt = ent["tt"]
                e.wait_ge(sm["x"].h, ent["t_x"])
                e.reduce_sum(stc(ST_XSUM, tt), xbuf[tt % 2][:],
                             axis=AX.X).then_inc(sm["vs"].h, 1)
            e.wait_ge(sm["sq"].h, T_SQ_FENCE)
            e.wait_ge(sm["vs"].h, a1[-1]["t_vs"])
            stp = [a1[-1]["t_vs"]]
            def step(ins):
                stp[0] = sm["vs"].tag() if False else stp[0] + 1
                ins.then_inc(sm["vs"].h, 1)
                e.wait_ge(sm["vs"].h, stp[0])
            step(e.tensor_scalar_mul(str_(ST_MEAN, 0, NT), str_(ST_XSUM, 0, NT), 1.0 / H))
            step(e.tensor_scalar_mul(str_(ST_EX2, 0, NT), str_(ST_SSUM, 0, NT), 1.0 / H))
            step(e.tensor_mul(str_(ST_TMP, 0, NT), str_(ST_MEAN, 0, NT), str_(ST_MEAN, 0, NT)))
            step(e.tensor_sub(str_(ST_VAR, 0, NT), str_(ST_EX2, 0, NT), str_(ST_TMP, 0, NT)))
            step(e.tensor_scalar_add(str_(ST_VAR, 0, NT), str_(ST_VAR, 0, NT), EPS))
            step(e.reciprocal(str_(ST_VAR, 0, NT), str_(ST_VAR, 0, NT)))
            e.tensor_copy(str_(ST_SSUM, 32, NT), str_(ST_VAR, 0, NT)).then_inc(sm["st1"].h, 1)
            e.wait_ge(sm["st2"].h, 1)
            step(e.scalar_tensor_tensor(str_(ST_NMR, 0, NT), str_(ST_MEAN, 0, NT), -1.0,
                                        str_(ST_RSTD, 0, NT), op0=ALU.mult, op1=ALU.mult))
            e.engine_nop() if False else None
            e.tensor_copy(str_(ST_SSUM, 32, NT), str_(ST_NMR, 0, NT)).then_inc(sm["st3"].h, 1)
            if upto == "a1":
                return
            e.wait_ge(sm["set"].h, SET_AB)
            for h2 in range(2):
                for ent in [z for z in a2 if z["h2"] == h2]:
                    lt, gi2 = ent["lt"], ent["gi2"]
                    if h2 == 1 and lt == 0 and upto != "a2":
                        h0all = [z for z in b_sweeps if z["h2"] == 0]
                        if h0all:
                            e.wait_ge(sm["qkvp"].h, h0all[-1]["t_qkvp"])
                    for gtp in range(4):
                        e.wait_ge(sm["tp"].h, ent["t_tp"][gtp])
                        dst = bass.AP(ln1T[:].tensor,
                                      ln1T[:].offset + (4 * gtp) * T + lt * 128,
                                      [ln1T[:].ap[0], [T, 4], [1, 128]])
                        e.tensor_copy(dst, tp_ps[(ent["gi2"] * 4 + gtp) % 2][:].rearrange(
                            "p (k c) -> p k c", k=4)).then_inc(sm["tpf"].h, 1)
                for ent in ([] if upto == "a2" else
                            [z for z in b_sweeps if z["h2"] == h2 and z["wi"] != 2]):
                    wi, oc, g = ent["wi"], ent["oc"], ent["g"]
                    cs, sn = (cosq_t, sinq_t) if wi == 0 else (cosk_t, sink_t)
                    bias_t = bt["bq"] if wi == 0 else bt["bk"]
                    e.wait_ge(sm["qkvp"].h, ent["t_qkvp"])
                    if g >= 2:
                        e.wait_ge(sm["sp"].h, b_sweeps[g - 2]["t_sp"])
                    for ti in range(4):
                        col0 = ti * 512
                        tcol = (h2 * T if wi == 1 else 0) + col0
                        bank = pg[(g % 2) * 4 + ti]
                        ro = ent["rt"][ti]
                        if upto == "bqm":
                            ins = e.tensor_copy(spill[g % 2][:, col0:col0 + 512], bank[:])
                            ins.then_inc(sm["evr"].h if ti == 3 else sm["rtc"].h, 1)
                            continue
                        e.wait_ge(sm["rt"].h, ro)
                        e.tensor_tensor(ropea[:], ropeu[(ro // 2) % 2][:],
                                        cs[:, tcol:tcol + 512],
                                        op=ALU.mult).then_inc(sm["rtc"].h, 1)
                        e.tensor_tensor(ropeb[:], ropet[(ro // 2) % 2][:],
                                        sn[:, tcol:tcol + 512],
                                        op=ALU.mult).then_inc(sm["rtc"].h, 1)
                        e.wait_ge(sm["rtc"].h, ent["rtc"][ti])
                        ins = e.tensor_add(spill[g % 2][:, col0:col0 + 512],
                                           ropea[:], ropeb[:])
                        if ti == 3:
                            ins.then_inc(sm["evr"].h, 1)

        @blk.tensor
        def _(e):
            if upto == "a1":
                return
            e.wait_ge(sm["set"].h, SET_AB)
            for h2 in range(2):
                if h2 == 1 and upto != "a2":
                    h0r = [z for z in b_sweeps if z["h2"] == 0 and z["wi"] != 2]
                    h0v = [z for z in b_sweeps if z["h2"] == 0 and z["wi"] == 2]
                    if h0r:
                        e.wait_ge(sm["evr"].h, h0r[-1]["ev"][1])
                    if h0v:
                        e.wait_ge(sm["evv"].h, h0v[-1]["ev"][1])
                for ent in [z for z in a2 if z["h2"] == h2]:
                    gi2 = ent["gi2"]
                    e.wait_ge(sm["ln"].h, ent["t_ln"])
                    for gtp in range(4):
                        gi = gi2 * 4 + gtp
                        if gi >= 2:
                            pent = a2[(gi - 2) // 4]
                            e.wait_ge(sm["tpf"].h, pent["t_tpf"][(gi - 2) % 4])
                        for j in range(4):
                            kc = 4 * gtp + j
                            ins = e.transpose(tp_ps[gi % 2][:, j * 128:(j + 1) * 128],
                                              lnb[gi2 % 2][:, kc * 128:(kc + 1) * 128],
                                              ident_t[:])
                            if j == 3:
                                ins.then_inc(sm["tp"].h, 1)
                last_a2 = [z for z in a2 if z["h2"] == h2][-1]
                e.wait_ge(sm["tpf"].h, last_a2["t_tpf"][3])
                for ent in ([] if upto == "a2" else [z for z in b_sweeps if z["h2"] == h2]):
                    wi, oc, g = ent["wi"], ent["oc"], ent["g"]
                    e.wait_ge(sm["w"].h, ent["t_w"])
                    if g >= 2:
                        sem, tgt = b_sweeps[g - 2]["ev"]
                        e.wait_ge(sm[sem].h, tgt)
                    for kci in range(KC):
                        e.ldweights(wblk[g % 4][:, kci * 128:(kci + 1) * 128])
                        for ti in range(4):
                            ins = e.matmul(pg[(g % 2) * 4 + ti][:],
                                           wblk[g % 4][:, kci * 128:(kci + 1) * 128],
                                           ln1c(kci, ti * 512, 512),
                                           start=(kci == 0), stop=(kci == KC - 1))
                            if kci == KC - 1 and ti == 3:
                                ins.then_inc(sm["qkvp"].h, 1)

    if upto in ("a1", "a2", "ln1", "qkv", "bq", "bq1", "bqm", "bqk", "bv"):
        return nc

    nc.sbuf_base = sbA
    kTb = [nc.alloc_sbuf_tensor(f"kTb{i}", [128, S], BF16) for i in range(2)]
    vTb = [nc.alloc_sbuf_tensor(f"vTb{i}", [128, S], BF16) for i in range(2)]
    qTb = [nc.alloc_sbuf_tensor(f"qTb{i}", [128, T], BF16) for i in range(2)]
    vh_t = nc.alloc_sbuf_tensor("vh_t", [128, S], BF16)
    expb = [nc.alloc_sbuf_tensor(f"expb{i}", [128, 1024], BF16) for i in range(4)]
    dacc = [nc.alloc_sbuf_tensor(f"dacc{i}", [128, 1024], BF16) for i in range(4)]
    dc1 = nc.alloc_sbuf_tensor("dc1", [128, 1024], BF16)
    dc2 = nc.alloc_sbuf_tensor("dc2", [128, 1024], BF16)
    den_sb = nc.alloc_sbuf_tensor("den_sb", [128, 1024], F32)
    den2_sb = nc.alloc_sbuf_tensor("den2_sb", [128, 1024], F32)
    recip_sb = nc.alloc_sbuf_tensor("recip_sb", [128, 1024], F32)
    aev = [nc.alloc_sbuf_tensor(f"aev{i}", [128, 1024], BF16) for i in range(2)]

    nc.psum_base = psA
    scps = [nc.alloc_psum_tensor(f"scps{i}", [128, 1024], F32) for i in range(2)]
    outps = [nc.alloc_psum_tensor(f"outps{i}", [128, 1024], F32) for i in range(2)]
    nc.psum_base = psA
    denps = nc.alloc_psum_tensor("denps", [128, 1024], F32)
    nc.psum_base = psA
    tpv_ps = [nc.alloc_psum_tensor(f"tpv{i}", [128, 512], BF16) for i in range(4)]
    nc.psum_base = psA + 8

    B_QKVP = sm["qkvp"].n
    B_EVR, B_EVV, B_SP = sm["evr"].n, sm["evv"].n, sm["sp"].n

    with nc.Block("C") as blk:
        @blk.sync
        def _(e):
            e.wait_ge(sm["qkvp"].h, B_QKVP)
            e.wait_ge(sm["evr"].h, B_EVR)
            e.wait_ge(sm["evv"].h, B_EVV)
            e.wait_ge(sm["sp"].h, B_SP)
            def spill_aev(hq):
                q2 = chq(hq)
                e.wait_ge(sm["oev"].h, q2["t_oev"])
                e.dma_start(out=attn_s[q2["h"], :,
                                       q2["qb"] * 1024:(q2["qb"] + 1) * 1024],
                            in_=aev[hq % 2][:]).then_inc(sm["aev"].h, 16)
            for q in c_hq:
                if q["qb"] == 1:
                    continue
                h = q["h"]
                if h >= 2:
                    e.wait_ge(sm["hd"].h, chq((h - 2) * 2 + 1)["t_hd"])
                e.dma_start(out=kTb[h % 2][:], in_=kT_s[h, :, :]).then_inc(sm["kvq"].h, 16)
                e.dma_start(out=vTb[h % 2][:], in_=vT_s[h, :, :]).then_inc(sm["kvq"].h, 16)
                e.dma_start(out=qTb[h % 2][:], in_=qT_s[h, :, :]).then_inc(sm["kvq"].h, 16)
                if h >= 1:
                    spill_aev((h - 1) * 2)
                    spill_aev((h - 1) * 2 + 1)
            spill_aev((NHC - 1) * 2)
            spill_aev((NHC - 1) * 2 + 1)
            if upto in ("attn", "c1", "c2", "c4"):
                e.wait_ge(sm["aev"].h, sm["aev"].n)

        @blk.tensor
        def _(e):
            e.wait_ge(sm["evr"].h, B_EVR)
            e.wait_ge(sm["evv"].h, B_EVV)
            for q in c_hq:
                h, qb, hq = q["h"], q["qb"], q["hq"]
                if qb == 0:
                    e.wait_ge(sm["kvq"].h, q["t_kvq"])
                    if h > 0:
                        e.wait_ge(sm["exp"].h, chq(h * 2 - 1)["t_exp"][31])
                        e.wait_ge(sm["den"].h, chq(h * 2 - 1)["t_den"] - 1)
                    for gj in range(8):
                        if gj >= 2:
                            e.wait_ge(sm["tpvf"].h, q["t_tpvf"][gj - 2])
                        for j in range(4):
                            kc = gj * 4 + j
                            ins = e.transpose(tpv_ps[gj % 2][:, j * 128:(j + 1) * 128],
                                              vTb[h % 2][:, kc * 128:(kc + 1) * 128],
                                              ident_t[:])
                            if j == 3:
                                ins.then_inc(sm["tpv"].h, 1)
                    e.wait_ge(sm["tpvf"].h, q["t_tpvf"][7])
                def emit_scores(kc):
                    gexp = hq * 32 + kc
                    if kc == 0 and hq >= 1:
                        e.wait_ge(sm["den"].h, chq(hq - 1)["t_den"] - 1)
                    if qb == 0 and kc < 2:
                        pass
                    elif gexp >= 2:
                        pq, pkc = divmod(gexp - 2, 32)
                        e.wait_ge(sm["exp"].h, chq(pq)["t_exp"][pkc])
                    e.ldweights(kTb[h % 2][:, kc * 128:(kc + 1) * 128])
                    for half in range(2):
                        ins = e.matmul(scps[kc % 2][:, half * 512:(half + 1) * 512],
                                       kTb[h % 2][:, kc * 128:(kc + 1) * 128],
                                       qTb[h % 2][:, qb * 1024 + half * 512:
                                                   qb * 1024 + (half + 1) * 512],
                                       start=True, stop=True)
                        if half == 1:
                            ins.then_inc(sm["sc"].h, 1)

                def emit_v(kc):
                    e.wait_ge(sm["exp"].h, q["t_exp"][kc])
                    if kc == 0 and hq >= 2:
                        e.wait_ge(sm["oev"].h, chq(hq - 2)["t_oev"])
                    e.ldweights(vh_t[:, kc * 128:(kc + 1) * 128])
                    for half in range(2):
                        ins = e.matmul(outps[hq % 2][:, half * 512:(half + 1) * 512],
                                       vh_t[:, kc * 128:(kc + 1) * 128],
                                       expb[kc % 4][:, half * 512:(half + 1) * 512],
                                       start=(kc == 0), stop=(kc == 31))
                        if half == 1:
                            ins.then_inc(sm["vm"].h, 1)

                for kc in range(32):
                    emit_scores(kc)
                    if kc >= 1:
                        emit_v(kc - 1)
                emit_v(31)
                e.wait_ge(sm["dng"].h, q["t_dncmb"])
                if hq >= 1:
                    e.wait_ge(sm["den"].h, chq(hq - 1)["t_den"])
                for half in range(2):
                    ins = e.matmul(denps[:, half * 512:(half + 1) * 512],
                                   ones128_t[:],
                                   den_sb[:, half * 512:(half + 1) * 512],
                                   start=True, stop=True)
                    if half == 1:
                        ins.then_inc(sm["denb"].h, 1)
                if qb == 1:
                    e.nop().then_inc(sm["hd"].h, 1)

        @blk.scalar
        def _(e):
            e.wait_ge(sm["setc"].h, 2)
            for q in c_hq:
                for kc in range(32):
                    gexp = q["hq"] * 32 + kc
                    e.wait_ge(sm["sc"].h, q["t_sc"][kc])
                    if gexp >= 4:
                        pq, pkc = divmod(gexp - 4, 32)
                        e.wait_ge(sm["vm"].h, chq(pq)["t_vm"][pkc])
                        e.wait_ge(sm["dng"].h,
                                  chq(pq)["t_dng"][pkc // 4] - 3 + pkc % 4)
                    e.activation(expb[kc % 4][:], scps[kc % 2][:], AF.Exp,
                                 bias=consts_t[:, 0:1]).then_inc(sm["exp"].h, 1)

        @blk.vector
        def _(e):
            for q in c_hq:
                h, qb, hq = q["h"], q["qb"], q["hq"]
                if qb == 0:
                    for gj in range(8):
                        e.wait_ge(sm["tpv"].h, q["t_tpv"][gj])
                        e.tensor_copy(vh_t[:, gj * 512:(gj + 1) * 512],
                                      tpv_ps[gj % 2][:]).then_inc(sm["tpvf"].h, 1)
                for gd in range(8):
                    for bidx in range(4):
                        kc = 4 * gd + bidx
                        e.wait_ge(sm["exp"].h, q["t_exp"][kc])
                        if gd == 0:
                            ins = e.tensor_copy(dacc[bidx][:], expb[bidx][:])
                        else:
                            ins = e.tensor_add(dacc[bidx][:], dacc[bidx][:],
                                               expb[bidx][:])
                        ins.then_inc(sm["dng"].h, 1)
                e.wait_ge(sm["dng"].h, q["t_dng"][7])
                e.tensor_add(dc1[:], dacc[0][:], dacc[1][:]).then_inc(sm["dng"].h, 1)
                e.tensor_add(dc2[:], dacc[2][:], dacc[3][:]).then_inc(sm["dng"].h, 1)
                e.wait_ge(sm["dng"].h, q["t_dncmb"] - 1)
                e.tensor_add(den_sb[:], dc1[:], dc2[:]).then_inc(sm["dng"].h, 1)
                e.wait_ge(sm["denb"].h, q["t_denb"])
                e.tensor_copy(den2_sb[:], denps[:]).then_inc(sm["den"].h, 1)
                e.wait_ge(sm["den"].h, q["t_den"] - 1)
                e.reciprocal(recip_sb[:], den2_sb[:]).then_inc(sm["den"].h, 1)
                e.wait_ge(sm["den"].h, q["t_den"])
                e.wait_ge(sm["vm"].h, q["t_vm"][31])
                if hq >= 2:
                    e.wait_ge(sm["aev"].h, chq(hq - 2)["t_aev"])
                e.tensor_tensor(aev[hq % 2][:], outps[hq % 2][:], recip_sb[:],
                                op=ALU.mult).then_inc(sm["oev"].h, 1)

    if upto in ("attn", "c1", "c2", "c4"):
        return nc

    sbC = nc.sbuf_base
    nc.sbuf_base = sbA
    woT_sb = nc.alloc_sbuf_tensor("woT_sb", [128, OC * H], BF16)
    attnD = [nc.alloc_sbuf_tensor(f"attnD{i}", [128, OC * 128], BF16) for i in range(2)]
    x2b = [nc.alloc_sbuf_tensor(f"x2b{i}", [128, H], F32) for i in range(2)]

    nc.psum_base = psA
    yps = [nc.alloc_psum_tensor(f"yps{i}", [128, 512], F32) for i in range(4)]
    tp2_ps = [nc.alloc_psum_tensor(f"tp2_{i}", [128, 512], BF16) for i in range(2)]

    C_OEV, C_AEV, C_HD = sm["oev"].n, sm["aev"].n, sm["hd"].n

    with nc.Block("D") as blk:
        @blk.sync
        def _(e):
            def spill_x2(qc):
                e.wait_ge(sm["sq2"].h, d_qc[qc]["t_sq2"])
                e.dma_start(out=x2_s[qc, :, :],
                            in_=x2b[qc % 2][:]).then_inc(sm["spx2"].h, 16)
            for ent in d_qc:
                qc = ent["qc"]
                if qc >= 2:
                    e.wait_ge(sm["x2"].h, d_qc[qc - 2]["t_x2"])
                e.dma_start(out=xbuf[qc % 2][:],
                            in_=x_d[qc * 128:(qc + 1) * 128, :]).then_inc(sm["xr"].h, 16)
                if qc >= 1:
                    spill_x2(qc - 1)
            spill_x2(NQ - 1)
            for ent in d2_qc:
                qc = ent["qc"]
                e.wait_ge(sm["tpf2"].h, ent["t_tpf2"][3])
                dst = bass.AP(ln2T_s.tensor, ln2T_s.offset + qc * 128,
                              [[T, 128], [128 * T, KC], [1, 128]])
                e.dma_start(out=dst,
                            in_=spill[qc % 2][:, 0:KC * 128].rearrange(
                                "p (k c) -> p k c", k=KC)).then_inc(sm["spl2"].h, 16)

        @blk.gpsimd
        def _(e):
            e.wait_ge(sm["oev"].h, C_OEV)
            e.wait_ge(sm["aev"].h, C_AEV)
            for cc in range(OC):
                e.dma_start(
                    out=woT_sb[:, cc * H:(cc + 1) * H].rearrange(
                        "p (o c) -> p o c", o=4),
                    in_=w_d["wot"][cc * 128:(cc + 1) * 128, :].rearrange(
                        "p (o c) -> p o c", c=512),
                ).then_inc(sm["wo"].h, 16)
            for ent in d_qc:
                qc = ent["qc"]
                if qc >= 2:
                    e.wait_ge(sm["y"].h, d_qc[qc - 2]["t_y"])
                src = bass.AP(attn_s.tensor, attn_s.offset + qc * 128,
                              [[T, 128], [128 * T, NH], [1, 128]])
                e.dma_start(out=attnD[qc % 2][:].rearrange("p (k c) -> p k c", k=NH),
                            in_=src).then_inc(sm["attnl"].h, 16)
            for ent in d2_qc:
                qc = ent["qc"]
                if qc >= 2:
                    e.wait_ge(sm["ln2"].h, d2_qc[qc - 2]["t_ln2"])
                else:
                    e.wait_ge(sm["x2"].h, d_qc[NQ - 2 + qc]["t_x2"])
                e.wait_ge(sm["spx2"].h, d_qc[qc]["t_spx2"])
                e.dma_start(out=xbuf[qc % 2][:],
                            in_=x2_s[qc, :, :]).then_inc(sm["x2r"].h, 16)

        @blk.tensor
        def _(e):
            e.wait_ge(sm["oev"].h, C_OEV)
            for ent in d_qc:
                qc = ent["qc"]
                e.wait_ge(sm["attnl"].h, ent["t_attnl"])
                if qc >= 1:
                    e.wait_ge(sm["x2"].h, d_qc[qc - 1]["t_x2"])
                for cc in range(OC):
                    if qc == 0:
                        e.wait_ge(sm["wo"].h, t_wo[cc])
                    e.ldweights(attnD[qc % 2][:, cc * 128:(cc + 1) * 128])
                    for ot in range(4):
                        ins = e.matmul(yps[ot][:],
                                       attnD[qc % 2][:, cc * 128:(cc + 1) * 128],
                                       woT_sb[:, cc * H + ot * 512:
                                              cc * H + (ot + 1) * 512],
                                       start=(cc == 0), stop=(cc == OC - 1))
                        if cc == OC - 1 and ot == 3:
                            ins.then_inc(sm["y"].h, 1)
            for ent in d2_qc:
                qc = ent["qc"]
                e.wait_ge(sm["ln2"].h, ent["t_ln2"])
                for g in range(4):
                    gi = qc * 4 + g
                    if gi >= 2:
                        pent = d2_qc[(gi - 2) // 4]
                        e.wait_ge(sm["tpf2"].h, pent["t_tpf2"][(gi - 2) % 4])
                    for j in range(4):
                        kc = 4 * g + j
                        ins = e.transpose(tp2_ps[gi % 2][:, j * 128:(j + 1) * 128],
                                          lnb[qc % 2][:, kc * 128:(kc + 1) * 128],
                                          ident_t[:])
                        if j == 3:
                            ins.then_inc(sm["tp2"].h, 1)

        @blk.scalar
        def _(e):
            for ent in d_qc:
                qc = ent["qc"]
                e.wait_ge(sm["x2"].h, ent["t_x2"])
                e.activation(sqtmp[:], x2b[qc % 2][:], AF.Square,
                             accum_out=stc(ST_SSUM, 32 + qc)).then_inc(sm["sq2"].h, 1)
            e.copy(str_(ST_TMP, 48, NQ), str_(ST_SSUM, 32, NQ)).then_inc(sm["sq2"].h, 1)
            e.wait_ge(sm["st21"].h, 1)
            e.activation(str_(ST_RSTD, 32, NQ), str_(ST_VAR, 32, NQ),
                         AF.Sqrt).then_inc(sm["st22"].h, 1)
            e.wait_ge(sm["st23"].h, 1)
            for ent in d2_qc:
                qc = ent["qc"]
                e.wait_ge(sm["x2r"].h, ent["t_x2r"])
                if qc >= 2:
                    e.wait_ge(sm["tp2"].h, d2_qc[qc - 2]["t_tp2"][3])
                e.activation(lnb[qc % 2][:], xbuf[qc % 2][:], AF.Identity,
                             bias=stc(ST_NMR, 32 + qc),
                             scale=stc(ST_RSTD, 32 + qc)).then_inc(sm["ln2"].h, 1)

        @blk.vector
        def _(e):
            for ent in d_qc:
                qc = ent["qc"]
                e.wait_ge(sm["y"].h, ent["t_y"])
                e.wait_ge(sm["xr"].h, ent["t_xr"])
                if qc >= 2:
                    e.wait_ge(sm["spx2"].h, d_qc[qc - 2]["t_spx2"])
                    e.wait_ge(sm["sq2"].h, d_qc[qc - 2]["t_sq2"])
                for ot in range(4):
                    ins = e.tensor_add(x2b[qc % 2][:, ot * 512:(ot + 1) * 512],
                                       yps[ot][:],
                                       xbuf[qc % 2][:, ot * 512:(ot + 1) * 512])
                    if ot == 3:
                        ins.then_inc(sm["x2"].h, 1)
                e.wait_ge(sm["x2"].h, ent["t_x2"])
                e.reduce_sum(stc(ST_XSUM, 32 + qc), x2b[qc % 2][:],
                             axis=AX.X).then_inc(sm["vs2"].h, 1)
            e.wait_ge(sm["sq2"].h, T_SQ2_FENCE)
            e.wait_ge(sm["vs2"].h, d_qc[-1]["t_vs2"])
            stp2 = [d_qc[-1]["t_vs2"]]
            def step2(ins):
                stp2[0] += 1
                ins.then_inc(sm["vs2"].h, 1)
                e.wait_ge(sm["vs2"].h, stp2[0])
            step2(e.tensor_scalar_mul(str_(ST_MEAN, 32, NQ), str_(ST_XSUM, 32, NQ), 1.0 / H))
            step2(e.tensor_scalar_mul(str_(ST_EX2, 32, NQ), str_(ST_SSUM, 32, NQ), 1.0 / H))
            step2(e.tensor_mul(str_(ST_TMP, 32, NQ), str_(ST_MEAN, 32, NQ), str_(ST_MEAN, 32, NQ)))
            step2(e.tensor_sub(str_(ST_VAR, 32, NQ), str_(ST_EX2, 32, NQ), str_(ST_TMP, 32, NQ)))
            step2(e.tensor_scalar_add(str_(ST_VAR, 32, NQ), str_(ST_VAR, 32, NQ), EPS))
            step2(e.reciprocal(str_(ST_VAR, 32, NQ), str_(ST_VAR, 32, NQ)))
            e.tensor_copy(str_(ST_SSUM, 48, NQ), str_(ST_VAR, 32, NQ)).then_inc(sm["st21"].h, 1)
            e.wait_ge(sm["st22"].h, 1)
            step2(e.scalar_tensor_tensor(str_(ST_NMR, 32, NQ), str_(ST_MEAN, 32, NQ), -1.0,
                                         str_(ST_RSTD, 32, NQ), op0=ALU.mult, op1=ALU.mult))
            e.tensor_copy(str_(ST_SSUM, 48, NQ), str_(ST_NMR, 32, NQ)).then_inc(sm["st23"].h, 1)
            for ent in d2_qc:
                qc = ent["qc"]
                for g in range(4):
                    gi = qc * 4 + g
                    e.wait_ge(sm["tp2"].h, ent["t_tp2"][g])
                    if qc >= 2 and g == 0:
                        e.wait_ge(sm["spl2"].h, d2_qc[qc - 2]["t_spl2"])
                    e.tensor_copy(spill[qc % 2][:, g * 512:(g + 1) * 512],
                                  tp2_ps[gi % 2][:]).then_inc(sm["tpf2"].h, 1)

    if upto == "wo":
        with nc.Block("Dend") as blk2:
            @blk2.sync
            def _(e):
                e.wait_ge(sm["spl2"].h, sm["spl2"].n)
                e.wait_ge(sm["spx2"].h, sm["spx2"].n)
        return nc

    nc.sbuf_base = sbA
    ln2q = [nc.alloc_sbuf_tensor(f"ln2q{i}", [128, KC * 512], BF16)
            for i in range(2)]
    h_sb = nc.alloc_sbuf_tensor("h_sb", [128, FC * 512], BF16)
    o0_sb = nc.alloc_sbuf_tensor("o0_sb", [128, OC * 512], BF16)
    mask_u8 = nc.alloc_sbuf_tensor("mask_u8", [128, 512], U8)
    mrow_sb = nc.alloc_sbuf_tensor("mrow_sb", [1, 512], BF16)
    gcp_sb = nc.alloc_sbuf_tensor("gcp_sb", [1, 512], F32)
    s1tmp = [nc.alloc_sbuf_tensor(f"s1tmp{i}", [128, 512], BF16) for i in range(2)]
    tmp0 = nc.alloc_sbuf_tensor("tmp0", [128, 512], BF16)
    finalb = [nc.alloc_sbuf_tensor(f"finalb{i}", [128, H], F32) for i in range(2)]

    nc.psum_base = psA
    hps = [nc.alloc_psum_tensor(f"hps{i}", [128, 512], F32) for i in range(2)]
    ops = [nc.alloc_psum_tensor(f"ops{i}", [128, 512], F32) for i in range(2)]
    psMid = nc.psum_base
    gps = nc.alloc_psum_tensor("gps", [1, 512], F32)
    mskps = nc.alloc_psum_tensor("mskps", [128, 512], F32)
    nc.psum_base = psMid
    psf = [nc.alloc_psum_tensor(f"psf{i}", [128, 2048], BF16) for i in range(2)]

    D_Y, D_SPL2, D_LN2 = sm["y"].n, sm["spl2"].n, sm["ln2"].n
    wE = {0: "w0at", 1: "w0bt", 2: "w1at", 3: "w1bt"}

    def eprev(bb, back):
        i = bb["stream_i"] - back
        return e_blocks[i] if i >= 0 else None

    with nc.Block("E") as blk:
        @blk.sync
        def _(e):
            e.wait_ge(sm["y"].h, D_Y)
            e.wait_ge(sm["spl2"].h, D_SPL2)
            def load_ln2q(qq):
                if qq >= 2:
                    e.wait_ge(sm["hp"].h, e_qq[qq - 2]["ph"][2]["blk"][FC - 1]["t_hp"])
                    e.wait_ge(sm["gate"].h, e_qq[qq - 2]["t_gate"])
                for kc in range(KC):
                    e.dma_start(out=ln2q[qq % 2][:, kc * 512:(kc + 1) * 512],
                                in_=ln2T_s[kc, :, qq * 512:(qq + 1) * 512]
                                ).then_inc(sm["lnq"].h, 16)
            load_ln2q(0)
            load_ln2q(1)
            for ent in e_qq:
                qq = ent["qq"]
                if qq + 2 < 4:
                    load_ln2q(qq + 2)
                for f in ent["fin"]:
                    qcB = f["qcB"]
                    e.wait_ge(sm["fin"].h, f["t_fin"])
                    e.dma_start(out=out_d[qq * 512 + qcB * 128:
                                          qq * 512 + (qcB + 1) * 128, :],
                                in_=finalb[qcB % 2][:]).then_inc(sm["outd"].h, 16)
            e.wait_ge(sm["outd"].h, sm["outd"].n)

        @blk.gpsimd
        def _(e):
            e.wait_ge(sm["y"].h, D_Y)
            e.wait_ge(sm["ln2"].h, D_LN2)
            for ent in e_qq:
                qq = ent["qq"]
                for p in ent["ph"]:
                    pi = p["pi"]
                    wd = w_d[wE[pi]]
                    nch = KC if pi in (0, 2) else FC
                    for j, bb in enumerate(p["blk"]):
                        pb = eprev(bb, 4)
                        if pb is not None:
                            e.wait_ge(sm["hp"].h if "t_hp" in pb else sm["op"].h,
                                      pb.get("t_hp") or pb.get("t_op"))
                        e.dma_start(
                            out=wblk[bb["stream_i"] % 4][:, 0:nch * 128].rearrange(
                                "p (k c) -> p k c", k=nch),
                            in_=wd[:, j * 128:(j + 1) * 128].rearrange(
                                "(k p) c -> p k c", p=128),
                        ).then_inc(sm["w2"].h, 16)
                for f in ent["fin"]:
                    qcB, gqc = f["qcB"], f["gqc"]
                    if gqc >= 2:
                        pf = e_qq[(gqc - 2) // 4]["fin"][(gqc - 2) % 4]
                        e.wait_ge(sm["fin"].h, pf["t_fin"])
                    e.dma_start(out=xbuf[qcB % 2][:],
                                in_=x2_s[gqc, :, :]).then_inc(sm["xe"].h, 16)

        @blk.tensor
        def _(e):
            e.wait_ge(sm["tpf2"].h, sm["tpf2"].n)
            for ent in e_qq:
                qq = ent["qq"]
                e.wait_ge(sm["lnq"].h, ent["t_lnq"])
                e.wait_ge(sm["setc"].h, 2)
                if qq >= 1:
                    e.wait_ge(sm["fin"].h, e_qq[qq - 1]["fin"][3]["t_fin"])
                    e.wait_ge(sm["mskf"].h, e_qq[qq - 1]["t_mskf"])
                for kc in range(KC):
                    ins = e.matmul(gps[0:1, :], wgd_t[:, kc:kc + 1],
                                   ln2q[qq % 2][:, kc * 512:(kc + 1) * 512],
                                   start=(kc == 0), stop=(kc == KC - 1))
                    if kc == KC - 1:
                        ins.then_inc(sm["gate"].h, 1)
                e.wait_ge(sm["msk"].h, ent["t_msk"])
                e.matmul(mskps[:], onesrow_t[:], mrow_sb[0:1, :],
                         start=True, stop=True).then_inc(sm["mskp"].h, 1)
                for p in ent["ph"]:
                    pi = p["pi"]
                    hpass = pi in (0, 2)
                    ncha = KC if hpass else FC
                    for j, bb in enumerate(p["blk"]):
                        e.wait_ge(sm["w2"].h, bb["t_w2"])
                        fam_prev = None
                        bi = bb["stream_i"] - 1
                        cnt = 0
                        while bi >= 0:
                            cb = e_blocks[bi]
                            if (cb["pi"] in (0, 2)) == hpass:
                                cnt += 1
                                if cnt == 2:
                                    fam_prev = cb
                                    break
                            bi -= 1
                        if fam_prev is not None:
                            if "t_hev" in fam_prev:
                                e.wait_ge(sm["hev"].h, fam_prev["t_hev"])
                            elif "t_oevB" in fam_prev:
                                e.wait_ge(sm["oevB"].h, fam_prev["t_oevB"])
                            else:
                                e.wait_ge(sm["s1"].h, fam_prev["t_s1"])
                        if pi in (1, 3) and j == 0:
                            e.wait_ge(sm["hev"].h, ent["ph"][pi - 1]["blk"][FC - 1]["t_hev"])
                        if pi == 2 and j == 0:
                            e.wait_ge(sm["op"].h, ent["ph"][1]["blk"][OC - 1]["t_op"])
                        if pi == 3 and j == 0 and qq >= 1:
                            pass
                        ps = (hps if hpass else ops)[j % 2]
                        rhsrc = ln2q[qq % 2] if hpass else h_sb
                        for kci in range(ncha):
                            e.ldweights(wblk[bb["stream_i"] % 4][:,
                                        kci * 128:(kci + 1) * 128])
                            ins = e.matmul(ps[:],
                                           wblk[bb["stream_i"] % 4][:,
                                           kci * 128:(kci + 1) * 128],
                                           rhsrc[:, kci * 512:(kci + 1) * 512],
                                           start=(kci == 0), stop=(kci == ncha - 1))
                            if kci == ncha - 1:
                                ins.then_inc(sm["hp"].h if hpass else sm["op"].h, 1)
                for f in ent["fin"]:
                    qcB, gqc = f["qcB"], f["gqc"]
                    if qcB == 0:
                        e.wait_ge(sm["bl"].h, ent["ph"][3]["blk"][OC - 1]["t_bl"])
                        e.wait_ge(sm["s1"].h, ent["ph"][3]["blk"][OC - 1]["t_s1"])
                    if gqc >= 2:
                        pf = e_qq[(gqc - 2) // 4]["fin"][(gqc - 2) % 4]
                        e.wait_ge(sm["fin"].h, pf["t_fin"])
                    for ocj in range(OC):
                        ins = e.transpose(psf[gqc % 2][:, ocj * 128:(ocj + 1) * 128],
                                          o0_sb[:, ocj * 512 + qcB * 128:
                                                ocj * 512 + (qcB + 1) * 128],
                                          ident_t[:])
                        if ocj == OC - 1:
                            ins.then_inc(sm["tpe"].h, 1)

        @blk.scalar
        def _(e):
            for ent in e_qq:
                for p in ent["ph"]:
                    pi = p["pi"]
                    for j, bb in enumerate(p["blk"]):
                        if pi in (0, 2):
                            e.wait_ge(sm["hp"].h, bb["t_hp"])
                            func = AF.Gelu if pi == 0 else AF.Relu
                            bias = bt["b0a"] if pi == 0 else bt["b1a"]
                            e.activation(h_sb[:, j * 512:(j + 1) * 512],
                                         hps[j % 2][:], func,
                                         bias=bias[:, j:j + 1]).then_inc(sm["hev"].h, 1)
                        elif pi == 1:
                            e.wait_ge(sm["op"].h, bb["t_op"])
                            e.activation(o0_sb[:, j * 512:(j + 1) * 512],
                                         ops[j % 2][:], AF.Identity,
                                         bias=bt["b0b"][:, j:j + 1]).then_inc(sm["oevB"].h, 1)
                        else:
                            e.wait_ge(sm["op"].h, bb["t_op"])
                            if j >= 2:
                                e.wait_ge(sm["bl"].h, p["blk"][j - 2]["t_bl"])
                            e.activation(s1tmp[j % 2][:], ops[j % 2][:], AF.Identity,
                                         bias=bt["b1b"][:, j:j + 1]).then_inc(sm["s1"].h, 1)

        @blk.vector
        def _(e):
            for ent in e_qq:
                qq = ent["qq"]
                e.wait_ge(sm["gate"].h, ent["t_gate"])
                e.tensor_copy(gcp_sb[:], gps[0:1, :]).then_inc(sm["msk"].h, 1)
                e.wait_ge(sm["msk"].h, ent["t_mskcp"])
                e.tensor_single_scalar(mrow_sb[:], gcp_sb[:], 0.0,
                                       op=ALU.is_ge).then_inc(sm["msk"].h, 1)
                e.wait_ge(sm["mskp"].h, ent["t_mskp"])
                e.tensor_copy(mask_u8[:], mskps[:]).then_inc(sm["mskf"].h, 1)
                p3 = ent["ph"][3]
                for j, bb in enumerate(p3["blk"]):
                    e.wait_ge(sm["s1"].h, bb["t_s1"])
                    e.tensor_copy(tmp0[:], o0_sb[:, j * 512:(j + 1) * 512]
                                  ).then_inc(sm["bl"].h, 1)
                    e.wait_ge(sm["bl"].h, bb["t_bl"] - 1)
                    e.select(o0_sb[:, j * 512:(j + 1) * 512], mask_u8[:],
                             tmp0[:], s1tmp[j % 2][:]).then_inc(sm["bl"].h, 1)
                for f in ent["fin"]:
                    qcB, gqc = f["qcB"], f["gqc"]
                    e.wait_ge(sm["tpe"].h, f["t_tpe"])
                    e.wait_ge(sm["xe"].h, f["t_xe"])
                    if gqc >= 2:
                        pf = e_qq[(gqc - 2) // 4]["fin"][(gqc - 2) % 4]
                        e.wait_ge(sm["outd"].h, pf["t_out"])
                    for ot in range(4):
                        ins = e.tensor_add(finalb[qcB % 2][:, ot * 512:(ot + 1) * 512],
                                           psf[gqc % 2][:, ot * 512:(ot + 1) * 512],
                                           xbuf[qcB % 2][:, ot * 512:(ot + 1) * 512])
                        if ot == 3:
                            ins.then_inc(sm["fin"].h, 1)

    return nc


_CACHE = {}


def _tables():
    inv = (1.0 / (1000000.0 ** (np.arange(0, HD, 2, dtype=np.float32) / HD))
           ).astype(np.float32)
    pos = np.arange(S, dtype=np.float32)
    fr = np.outer(pos, inv).astype(np.float32)
    cosT = np.concatenate([np.cos(fr).T, np.cos(fr).T], 0)
    sinT = np.concatenate([np.sin(fr).T, np.sin(fr).T], 0)
    return cosT, sinT


def make_inmaps(inputs):
    X = np.asarray(inputs["L0_attn_hidden_states"])
    cosT, sinT = _tables()
    sc = np.float32(1.0 / np.sqrt(HD))
    shared = {
        "wqt": np.ascontiguousarray(np.asarray(inputs["L0_attn_Wq"]).T),
        "wkt": np.ascontiguousarray(np.asarray(inputs["L0_attn_Wk"]).T),
        "wvt": np.ascontiguousarray(np.asarray(inputs["L0_attn_Wv"]).T),
        "wot": np.ascontiguousarray(np.asarray(inputs["L0_attn_Wo"]).T),
        "w0at": np.ascontiguousarray(np.asarray(inputs["L0_ffn_W0a"]).T),
        "w0bt": np.ascontiguousarray(np.asarray(inputs["L0_ffn_W0b"]).T),
        "w1at": np.ascontiguousarray(np.asarray(inputs["L0_ffn_W1a"]).T),
        "w1bt": np.ascontiguousarray(np.asarray(inputs["L0_ffn_W1b"]).T),
        "bq": np.asarray(inputs["L0_attn_bq"]),
        "bk": np.asarray(inputs["L0_attn_bk"]),
        "bv": np.asarray(inputs["L0_attn_bv"]),
        "wgt": np.ascontiguousarray(np.asarray(inputs["L0_ffn_Wg"]).T),
        "b0a": np.asarray(inputs["L0_ffn_b0a"]),
        "b0b": np.asarray(inputs["L0_ffn_b0b"]),
        "b1a": np.asarray(inputs["L0_ffn_b1a"]),
        "b1b": np.asarray(inputs["L0_ffn_b1b"]),
        "ident": np.eye(128, dtype=ml_dtypes.bfloat16),
        "ones128": np.ones((128, 128), np.float32),
        "onesrow": np.ones((1, 128), ml_dtypes.bfloat16),
    }
    in_maps = []
    for c in range(8):
        b, half = c // 2, c % 2
        roll = -half * T
        in_maps.append(dict(shared, **{
            "x": np.ascontiguousarray(np.roll(X[b], roll, axis=0)),
            "cosq": (cosT[:, half * T:(half + 1) * T] * sc).astype(ml_dtypes.bfloat16),
            "sinq": (sinT[:, half * T:(half + 1) * T] * sc).astype(ml_dtypes.bfloat16),
            "cosk": np.ascontiguousarray(np.roll(cosT, roll, axis=1)
                                         ).astype(ml_dtypes.bfloat16),
            "sink": np.ascontiguousarray(np.roll(sinT, roll, axis=1)
                                         ).astype(ml_dtypes.bfloat16),
        }))
    return in_maps


def kernel(**inputs):
    if "full" not in _CACHE:
        _CACHE["full"] = build("full")
    nc = _CACHE["full"]
    in_maps = make_inmaps(inputs)
    res = run_bass_kernel_spmd(nc, in_maps, core_ids=list(range(8)))
    OUT = np.empty((B, S, H), np.float32)
    for c in range(8):
        b, half = c // 2, c % 2
        OUT[b, half * T:(half + 1) * T, :] = res.results[c]["out"]
    return OUT


# version 05
# speedup vs baseline: 1.1190x; 1.1190x over previous
import numpy as np
import ml_dtypes
import concourse.bass as bass
import concourse.mybir as mybir
from concourse.bass_utils import run_bass_kernel_spmd

F32 = mybir.dt.float32
BF16 = mybir.dt.bfloat16
U8 = mybir.dt.uint8
AF = mybir.ActivationFunctionType
AX = mybir.AxisListType
ALU = mybir.AluOpType

B, S, H, NH, HD, FF = 4, 4096, 2048, 16, 128, 4096
T = S // 2
KC, OC, FC = H // 128, H // 128, FF // 128
NT = S // 128
NQ = T // 128
NEGSTAB = -4.0
EPS = 1e-5


class SemT:
    def __init__(self, nc, name):
        self.h = nc.alloc_semaphore(name=name)
        self.n = 0

    def tag(self, k=1):
        self.n += k
        return self.n


def build(upto="full"):
    nc = bass.Bass("TRN2", target_bir_lowering=False, debug=False)
    dbg = upto != "full"

    x_d = nc.dram_tensor("x", [S, H], F32, kind="ExternalInput").ap()
    w_d = {nm: nc.dram_tensor(nm, shp, F32, kind="ExternalInput").ap()
           for nm, shp in [("wqt", [H, H]), ("wkt", [H, H]), ("wvt", [H, H]),
                           ("wot", [H, H]), ("w0at", [H, FF]), ("w0bt", [FF, H]),
                           ("w1at", [H, FF]), ("w1bt", [FF, H])]}
    b_d = {nm: nc.dram_tensor(nm, [n], F32, kind="ExternalInput").ap()
           for nm, n in [("bq", H), ("bk", H), ("bv", H), ("b0a", FF),
                         ("b0b", H), ("b1a", FF), ("b1b", H)]}
    wgt_d = nc.dram_tensor("wgt", [H, 2], F32, kind="ExternalInput").ap()
    cosq_d = nc.dram_tensor("cosq", [128, T], BF16, kind="ExternalInput").ap()
    sinq_d = nc.dram_tensor("sinq", [128, T], BF16, kind="ExternalInput").ap()
    cosk_d = nc.dram_tensor("cosk", [128, S], BF16, kind="ExternalInput").ap()
    sink_d = nc.dram_tensor("sink", [128, S], BF16, kind="ExternalInput").ap()
    ident_d = nc.dram_tensor("ident", [128, 128], BF16, kind="ExternalInput").ap()
    ones128_d = nc.dram_tensor("ones128", [128, 128], F32, kind="ExternalInput").ap()
    onesrow_d = nc.dram_tensor("onesrow", [1, 128], BF16, kind="ExternalInput").ap()
    out_d = nc.dram_tensor("out", [T, H], F32, kind="ExternalOutput").ap()

    def scr(name, shape, dtype, expose):
        kind = "ExternalOutput" if (dbg and expose) else "Internal"
        return nc.dram_tensor(name, shape, dtype, kind=kind).ap()

    qT_s = scr("qT_s", [NH, 128, T], BF16, upto in ("qkv", "c1", "c2", "c4", "fullx"))
    kT_s = scr("kT_s", [NH, 128, S], BF16, upto in ("qkv", "c1", "c2", "c4", "fullx"))
    vT_s = scr("vT_s", [NH, 128, S], BF16, upto in ("qkv", "c1", "c2", "c4", "fullx"))
    attn_s = scr("attn_s", [NH, 128, T], BF16, upto in ("attn", "c1", "c2", "c4", "fullx"))
    x2_s = scr("x2_s", [NQ, 128, H], F32, upto in ("wo", "fullx"))
    ln2T_s = scr("ln2T_s", [KC, 128, T], BF16, upto in ("wo", "fullx"))
    ln1_dbg = scr("ln1_dbg", [KC, 128, S], BF16, upto in ("ln1", "a2"))
    stats_dbg = scr("stats_dbg", [128, 8 * 64], F32, upto == "a2")

    A = nc.alloc_sbuf_tensor
    bt = {nm: A(nm + "_t", [128, OC], F32) for nm in ["bq", "bk", "bv", "b0b", "b1b"]}
    bt["b0a"] = A("b0a_t", [128, FC], F32)
    bt["b1a"] = A("b1a_t", [128, FC], F32)
    bqsw = A("bqsw", [128, OC], F32)
    bksw = A("bksw", [128, OC], F32)
    wg_t = A("wg_t", [128, 2 * KC], F32)
    wgd_t = A("wgd_t", [128, KC], BF16)
    ident_t = A("ident_t", [128, 128], BF16)
    ones128_t = A("ones128_t", [128, 128], F32)
    onesrow_t = A("onesrow_t", [1, 128], BF16)
    consts_t = A("consts_t", [128, 4], F32)
    stats_t = A("stats_t", [128, 8 * 64], F32)
    xbuf = [A(f"xbuf{i}", [128, H], F32) for i in range(2)]
    wblk = [A(f"wblk{i}", [128, FF], BF16) for i in range(4)]
    spill = [A(f"spill{i}", [128, T], BF16) for i in range(2)]
    lnb = [A(f"lnb{i}", [128, H], BF16) for i in range(2)]
    sqtmp = A("sqtmp", [128, H], BF16)

    ST_XSUM, ST_SSUM, ST_MEAN, ST_EX2, ST_VAR, ST_RSTD, ST_NMR, ST_TMP = range(8)

    def stc(kind, tile):
        return stats_t[:, kind * 64 + tile: kind * 64 + tile + 1]

    def str_(kind, t0, n):
        return stats_t[:, kind * 64 + t0: kind * 64 + t0 + n]

    SN = ["set", "setc",
          "x", "sq", "vs", "st1", "st2", "st3", "ln", "tp", "tpf",
          "w", "qkvp", "evr", "evv", "sp", "rt", "rtc",
          "kvq", "tpv", "tpvf", "sc", "exp", "vm", "dng", "denb", "den",
          "oev", "aev", "hd",
          "wo", "attnl", "y", "xr", "x2", "sq2", "vs2", "st21", "st22", "st23",
          "x2r", "ln2", "tp2", "tpf2", "spl2", "spx2",
          "lnq", "gate", "msk", "mskp", "mskf", "w2", "hp", "hev", "op", "oevB",
          "s1", "bl", "tpe", "xe", "fin", "outd", "dbg"]
    sm = {nm: SemT(nc, "s_" + nm) for nm in SN}

    a1 = []
    for tt in range(NT):
        a1.append(dict(tt=tt, t_x=sm["x"].tag(16), t_sq=sm["sq"].tag(),
                       t_vs=sm["vs"].tag()))
    T_SQ_FENCE = sm["sq"].tag()
    _ = [sm["vs"].tag() for _ in range(8)]
    a2 = []
    for h2 in range(2):
        for lt in range(16):
            e = dict(h2=h2, lt=lt, tt=h2 * 16 + lt, gi2=len(a2))
            e["t_x"] = sm["x"].tag(16)
            e["t_ln"] = sm["ln"].tag()
            e["t_tp"] = [sm["tp"].tag() for _ in range(4)]
            e["t_tpf"] = [sm["tpf"].tag() for _ in range(4)]
            a2.append(e)

    b_sweeps = []
    for h2 in range(2):
        wis = [0, 1, 2] if h2 == 0 else [1, 2]
        for wi in wis:
            for oc in range(OC):
                b_sweeps.append(dict(h2=h2, wi=wi, oc=oc))
    if upto == "bq":
        b_sweeps = [z for z in b_sweeps if z["wi"] == 0]
    elif upto in ("bq1", "bqm"):
        b_sweeps = [z for z in b_sweeps if z["wi"] == 0 and z["oc"] == 0]
    elif upto == "bqk":
        b_sweeps = [z for z in b_sweeps if z["wi"] != 2]
    elif upto == "bv":
        b_sweeps = [z for z in b_sweeps if z["wi"] == 2]
    for g, ent in enumerate(b_sweeps):
        ent["g"] = g
        ent["t_w"] = sm["w"].tag(16)
        ent["t_qkvp"] = sm["qkvp"].tag()
        if ent["wi"] in (0, 1):
            ent["ev"] = ("evr", sm["evr"].tag())
            ent["rt"] = [sm["rt"].tag(2) for _ in range(4)]
            ent["rtc"] = [sm["rtc"].tag(2) for _ in range(4)]
        else:
            ent["ev"] = ("evv", sm["evv"].tag())
        ent["t_sp"] = sm["sp"].tag(16)
    NB = len(b_sweeps)

    NHC = {"c1": 1, "c2": 2, "c4": 4}.get(upto, NH)
    c_hq = []
    for h in range(NHC):
        for qb in range(2):
            q = dict(h=h, qb=qb, hq=h * 2 + qb)
            if qb == 0:
                q["t_kvq"] = sm["kvq"].tag(48)
                q["t_tpv"] = [sm["tpv"].tag() for _ in range(8)]
                q["t_tpvf"] = [sm["tpvf"].tag() for _ in range(8)]
            q["t_sc"] = [sm["sc"].tag() for _ in range(32)]
            q["t_exp"] = [sm["exp"].tag() for _ in range(32)]
            q["t_vm"] = [sm["vm"].tag() for _ in range(32)]
            q["t_dng"] = [sm["dng"].tag(4) for _ in range(8)]
            _ = sm["dng"].tag(2)
            q["t_dncmb"] = sm["dng"].tag()
            q["t_denb"] = sm["denb"].tag()
            _ = sm["den"].tag()
            q["t_den"] = sm["den"].tag()
            q["t_oev"] = sm["oev"].tag()
            q["t_aev"] = sm["aev"].tag(16)
            if qb == 1:
                q["t_hd"] = sm["hd"].tag()
            c_hq.append(q)

    def chq(hq):
        return c_hq[hq]

    t_wo = [sm["wo"].tag(16) for _ in range(OC)]
    d_qc = []
    for qc in range(NQ):
        e = dict(qc=qc, t_attnl=sm["attnl"].tag(16), t_y=sm["y"].tag(),
                 t_xr=sm["xr"].tag(16), t_x2=sm["x2"].tag(),
                 t_sq2=sm["sq2"].tag(), t_vs2=sm["vs2"].tag(),
                 t_spx2=sm["spx2"].tag(16))
        d_qc.append(e)
    T_SQ2_FENCE = sm["sq2"].tag()
    _ = [sm["vs2"].tag() for _ in range(8)]
    d2_qc = []
    for qc in range(NQ):
        e = dict(qc=qc, t_x2r=sm["x2r"].tag(16), t_ln2=sm["ln2"].tag(),
                 t_tp2=[sm["tp2"].tag() for _ in range(4)],
                 t_tpf2=[sm["tpf2"].tag() for _ in range(4)],
                 t_spl2=sm["spl2"].tag(16))
        d2_qc.append(e)

    e_blocks = []
    e_qq = []
    for qq in range(4):
        e = dict(qq=qq)
        e["t_lnq"] = sm["lnq"].tag(16 * 16)
        e["t_gate"] = sm["gate"].tag()
        e["t_mskcp"] = sm["msk"].tag()
        e["t_msk"] = sm["msk"].tag()
        e["t_mskp"] = sm["mskp"].tag()
        e["t_mskf"] = sm["mskf"].tag()
        e["ph"] = []
        for pi, nblk in enumerate([FC, OC, FC, OC]):
            p = dict(pi=pi, blk=[])
            for j in range(nblk):
                bb = dict(qq=qq, pi=pi, j=j)
                bb["t_w2"] = sm["w2"].tag(16)
                if pi in (0, 2):
                    bb["t_hp"] = sm["hp"].tag()
                    bb["t_hev"] = sm["hev"].tag()
                else:
                    bb["t_op"] = sm["op"].tag()
                    if pi == 1:
                        bb["t_oevB"] = sm["oevB"].tag()
                    else:
                        bb["t_s1"] = sm["s1"].tag()
                        _ = sm["bl"].tag()
                        bb["t_bl"] = sm["bl"].tag()
                bb["stream_i"] = len(e_blocks)
                e_blocks.append(bb)
                p["blk"].append(bb)
            e["ph"].append(p)
        e["fin"] = []
        for qcB in range(4):
            f = dict(qcB=qcB, gqc=qq * 4 + qcB)
            f["t_tpe"] = sm["tpe"].tag()
            f["t_xe"] = sm["xe"].tag(16)
            f["t_fin"] = sm["fin"].tag()
            f["t_out"] = sm["outd"].tag(16)
            e["fin"].append(f)
        e_qq.append(e)

    with nc.Block("setup") as blk:
        @blk.sync
        def _(e):
            for dst, src in [(ident_t, ident_d), (ones128_t, ones128_d),
                             (onesrow_t, onesrow_d)]:
                e.dma_start(out=dst[:], in_=src[:]).then_inc(sm["set"].h, 16)

        @blk.gpsimd
        def _(e):
            e.memset(consts_t[:, 0:1], NEGSTAB).then_inc(sm["setc"].h, 1)
            for nm in ["bq", "bk", "bv", "b0b", "b1b", "b0a", "b1a"]:
                e.dma_start(out=bt[nm][:], in_=b_d[nm].rearrange("(o p) -> p o", p=128),
                            allow_slow_non_contiguous=True).then_inc(sm["set"].h, 16)
            e.dma_start(out=wg_t[:].rearrange("p (k e) -> p k e", e=2),
                        in_=wgt_d.rearrange("(k p) e -> p k e", p=128),
                        allow_slow_non_contiguous=True).then_inc(sm["set"].h, 16)

        @blk.vector
        def _(e):
            e.wait_ge(sm["set"].h, 16 * 11)
            e.tensor_scalar_mul(bqsw[0:64, :], bt["bq"][64:128, :], -1.0)
            e.tensor_scalar_mul(bqsw[64:128, :], bt["bq"][0:64, :], 1.0)
            e.tensor_scalar_mul(bksw[0:64, :], bt["bk"][64:128, :], -1.0)
            e.tensor_scalar_mul(bksw[64:128, :], bt["bk"][0:64, :], 1.0)
            wg3 = wg_t[:].rearrange("p (k e) -> p k e", e=2)
            e.tensor_tensor(wgd_t[:], wg3[:, :, 0], wg3[:, :, 1],
                            op=ALU.subtract).then_inc(sm["setc"].h, 1)

    sm["setc"].n = 2
    SET_AB = 16 * 15

    if upto == "setup":
        with nc.Block("term") as blk:
            @blk.sync
            def _(e):
                e.wait_ge(sm["set"].h, 16 * 11)
                e.wait_ge(sm["setc"].h, 2)
        return nc

    sbA = nc.sbuf_base
    ln1T = nc.alloc_sbuf_tensor("ln1T", [128, KC * T], BF16)
    cosq_t = nc.alloc_sbuf_tensor("cosq_t", [128, T], BF16)
    sinq_t = nc.alloc_sbuf_tensor("sinq_t", [128, T], BF16)
    cosk_t = nc.alloc_sbuf_tensor("cosk_t", [128, S], BF16)
    sink_t = nc.alloc_sbuf_tensor("sink_t", [128, S], BF16)
    ropet = [nc.alloc_sbuf_tensor(f"ropet{i}", [128, 512], F32) for i in range(2)]
    ropeu = [nc.alloc_sbuf_tensor(f"ropeu{i}", [128, 512], F32) for i in range(2)]
    ropea = nc.alloc_sbuf_tensor("ropea", [128, 512], F32)
    ropeb = nc.alloc_sbuf_tensor("ropeb", [128, 512], F32)

    psA = nc.psum_base
    pg = [nc.alloc_psum_tensor(f"pg{i}", [128, 512], F32) for i in range(8)]
    nc.psum_base = psA
    tp_ps = [nc.alloc_psum_tensor(f"tp{i}", [128, 512], BF16) for i in range(2)]
    nc.psum_base = psA + 8

    def ln1c(kc, c0, n):
        return ln1T[:, kc * T + c0: kc * T + c0 + n]

    with nc.Block("AB") as blk:
        @blk.sync
        def _(e):
            for dst, src in [(cosq_t, cosq_d), (sinq_t, sinq_d),
                             (cosk_t, cosk_d), (sink_t, sink_d)]:
                e.dma_start(out=dst[:], in_=src[:]).then_inc(sm["set"].h, 16)
            for ent in a1:
                tt = ent["tt"]
                if tt >= 2:
                    e.wait_ge(sm["sq"].h, a1[tt - 2]["t_sq"])
                    e.wait_ge(sm["vs"].h, a1[tt - 2]["t_vs"])
                e.dma_start(out=xbuf[tt % 2][:],
                            in_=x_d[tt * 128:(tt + 1) * 128, :]).then_inc(sm["x"].h, 16)
            if upto == "a1":
                e.wait_ge(sm["st3"].h, 1)
                e.wait_ge(sm["st2"].h, 1)
                return
            for h2 in range(2):
                for ent in [z for z in a2 if z["h2"] == h2]:
                    tt, gi2 = ent["tt"], ent["gi2"]
                    if gi2 >= 2:
                        e.wait_ge(sm["ln"].h, a2[gi2 - 2]["t_ln"])
                    else:
                        e.wait_ge(sm["sq"].h, a1[NT - 1]["t_sq"])
                        e.wait_ge(sm["vs"].h, a1[NT - 1]["t_vs"])
                    e.dma_start(out=xbuf[gi2 % 2][:],
                                in_=x_d[tt * 128:(tt + 1) * 128, :]).then_inc(sm["x"].h, 16)
                for ent in ([] if upto == "a2" else [z for z in b_sweeps if z["h2"] == h2]):
                    sem, tgt = ent["ev"]
                    e.wait_ge(sm[sem].h, tgt)
                    wi, oc = ent["wi"], ent["oc"]
                    dest = [qT_s, kT_s, vT_s][wi]
                    col0 = 0 if wi == 0 else h2 * T
                    e.dma_start(out=dest[oc, :, col0:col0 + T],
                                in_=spill[ent["g"] % 2][:]).then_inc(sm["sp"].h, 16)
            if upto in ("qkv", "bq", "bq1", "bqm", "bqk", "bv"):
                e.wait_ge(sm["sp"].h, sm["sp"].n)
            if upto in ("ln1", "a2"):
                e.wait_ge(sm["tpf"].h, a2[-1]["t_tpf"][3])
                if upto == "a2":
                    e.dma_start(out=stats_dbg[:, :], in_=stats_t[:]).then_inc(sm["dbg"].h, 16)
                for kc in range(KC):
                    e.dma_start(out=ln1_dbg[kc, :, T:S], in_=ln1c(kc, 0, T)
                                ).then_inc(sm["dbg"].h, 16)
                e.wait_ge(sm["dbg"].h, 16 * KC)

        @blk.gpsimd
        def _(e):
            if upto in ("a1", "a2"):
                return
            wts = {0: w_d["wqt"], 1: w_d["wkt"], 2: w_d["wvt"]}
            for ent in b_sweeps:
                g, wi, oc = ent["g"], ent["wi"], ent["oc"]
                if g >= 4:
                    e.wait_ge(sm["qkvp"].h, b_sweeps[g - 4]["t_qkvp"])
                e.dma_start(
                    out=wblk[g % 4][:, 0:KC * 128].rearrange("p (k c) -> p k c", k=KC),
                    in_=wts[wi][:, oc * 128:(oc + 1) * 128].rearrange(
                        "(k p) c -> p k c", p=128),
                ).then_inc(sm["w"].h, 16)

        @blk.scalar
        def _(e):
            for ent in a1:
                tt = ent["tt"]
                e.wait_ge(sm["x"].h, ent["t_x"])
                e.activation(sqtmp[:], xbuf[tt % 2][:], AF.Square,
                             accum_out=stc(ST_SSUM, tt)).then_inc(sm["sq"].h, 1)
            e.copy(str_(ST_TMP, 32, NT), str_(ST_SSUM, 0, NT)).then_inc(sm["sq"].h, 1)
            e.wait_ge(sm["st1"].h, 1)
            e.activation(str_(ST_RSTD, 0, NT), str_(ST_VAR, 0, NT),
                         AF.Sqrt).then_inc(sm["st2"].h, 1)
            if upto == "a1":
                return
            e.wait_ge(sm["st3"].h, 1)
            e.wait_ge(sm["setc"].h, 2)
            for h2 in range(2):
                for ent in [z for z in a2 if z["h2"] == h2]:
                    tt, gi2 = ent["tt"], ent["gi2"]
                    e.wait_ge(sm["x"].h, ent["t_x"])
                    if gi2 >= 2:
                        e.wait_ge(sm["tp"].h, a2[gi2 - 2]["t_tp"][3])
                    e.activation(lnb[gi2 % 2][:], xbuf[gi2 % 2][:], AF.Identity,
                                 bias=stc(ST_NMR, tt),
                                 scale=stc(ST_RSTD, tt)).then_inc(sm["ln"].h, 1)
                for ent in ([] if upto == "a2" else [z for z in b_sweeps if z["h2"] == h2]):
                    wi, oc, g = ent["wi"], ent["oc"], ent["g"]
                    e.wait_ge(sm["qkvp"].h, ent["t_qkvp"])
                    if g >= 2:
                        e.wait_ge(sm["sp"].h, b_sweeps[g - 2]["t_sp"])
                    if wi == 2:
                        for ti in range(4):
                            ins = e.activation(spill[g % 2][:, ti * 512:(ti + 1) * 512],
                                               pg[(g % 2) * 4 + ti][:], AF.Identity,
                                               bias=bt["bv"][:, oc:oc + 1])
                            if ti == 3:
                                ins.then_inc(sm["evv"].h, 1)
                    elif upto != "bqm":
                        bias_t = bt["bq"] if wi == 0 else bt["bk"]
                        for ti in range(4):
                            ro = ent["rt"][ti]
                            if ro > 4:
                                e.wait_ge(sm["rtc"].h, ro - 4)
                            bank = pg[(g % 2) * 4 + ti]
                            ub = ropeu[(ro // 2) % 2]
                            buf = ropet[(ro // 2) % 2]
                            e.activation(ub[:], bank[:], AF.Identity,
                                         bias=bias_t[:, oc:oc + 1]).then_inc(sm["rt"].h, 1)
                            e.wait_ge(sm["rt"].h, ro - 1)
                            e.activation(buf[0:64, :], ub[64:128, :], AF.Identity,
                                         scale=-1.0)
                            e.activation(buf[64:128, :], ub[0:64, :],
                                         AF.Identity).then_inc(sm["rt"].h, 1)

        @blk.vector
        def _(e):
            for ent in a1:
                tt = ent["tt"]
                e.wait_ge(sm["x"].h, ent["t_x"])
                e.reduce_sum(stc(ST_XSUM, tt), xbuf[tt % 2][:],
                             axis=AX.X).then_inc(sm["vs"].h, 1)
            e.wait_ge(sm["sq"].h, T_SQ_FENCE)
            e.wait_ge(sm["vs"].h, a1[-1]["t_vs"])
            stp = [a1[-1]["t_vs"]]
            def step(ins):
                stp[0] = sm["vs"].tag() if False else stp[0] + 1
                ins.then_inc(sm["vs"].h, 1)
                e.wait_ge(sm["vs"].h, stp[0])
            step(e.tensor_scalar_mul(str_(ST_MEAN, 0, NT), str_(ST_XSUM, 0, NT), 1.0 / H))
            step(e.tensor_scalar_mul(str_(ST_EX2, 0, NT), str_(ST_SSUM, 0, NT), 1.0 / H))
            step(e.tensor_mul(str_(ST_TMP, 0, NT), str_(ST_MEAN, 0, NT), str_(ST_MEAN, 0, NT)))
            step(e.tensor_sub(str_(ST_VAR, 0, NT), str_(ST_EX2, 0, NT), str_(ST_TMP, 0, NT)))
            step(e.tensor_scalar_add(str_(ST_VAR, 0, NT), str_(ST_VAR, 0, NT), EPS))
            step(e.reciprocal(str_(ST_VAR, 0, NT), str_(ST_VAR, 0, NT)))
            e.tensor_copy(str_(ST_SSUM, 32, NT), str_(ST_VAR, 0, NT)).then_inc(sm["st1"].h, 1)
            e.wait_ge(sm["st2"].h, 1)
            step(e.scalar_tensor_tensor(str_(ST_NMR, 0, NT), str_(ST_MEAN, 0, NT), -1.0,
                                        str_(ST_RSTD, 0, NT), op0=ALU.mult, op1=ALU.mult))
            e.engine_nop() if False else None
            e.tensor_copy(str_(ST_SSUM, 32, NT), str_(ST_NMR, 0, NT)).then_inc(sm["st3"].h, 1)
            if upto == "a1":
                return
            e.wait_ge(sm["set"].h, SET_AB)
            for h2 in range(2):
                for ent in [z for z in a2 if z["h2"] == h2]:
                    lt, gi2 = ent["lt"], ent["gi2"]
                    if h2 == 1 and lt == 0 and upto != "a2":
                        h0all = [z for z in b_sweeps if z["h2"] == 0]
                        if h0all:
                            e.wait_ge(sm["qkvp"].h, h0all[-1]["t_qkvp"])
                    for gtp in range(4):
                        e.wait_ge(sm["tp"].h, ent["t_tp"][gtp])
                        dst = bass.AP(ln1T[:].tensor,
                                      ln1T[:].offset + (4 * gtp) * T + lt * 128,
                                      [ln1T[:].ap[0], [T, 4], [1, 128]])
                        e.tensor_copy(dst, tp_ps[(ent["gi2"] * 4 + gtp) % 2][:].rearrange(
                            "p (k c) -> p k c", k=4)).then_inc(sm["tpf"].h, 1)
                for ent in ([] if upto == "a2" else
                            [z for z in b_sweeps if z["h2"] == h2 and z["wi"] != 2]):
                    wi, oc, g = ent["wi"], ent["oc"], ent["g"]
                    cs, sn = (cosq_t, sinq_t) if wi == 0 else (cosk_t, sink_t)
                    bias_t = bt["bq"] if wi == 0 else bt["bk"]
                    e.wait_ge(sm["qkvp"].h, ent["t_qkvp"])
                    if g >= 2:
                        e.wait_ge(sm["sp"].h, b_sweeps[g - 2]["t_sp"])
                    for ti in range(4):
                        col0 = ti * 512
                        tcol = (h2 * T if wi == 1 else 0) + col0
                        bank = pg[(g % 2) * 4 + ti]
                        ro = ent["rt"][ti]
                        if upto == "bqm":
                            ins = e.tensor_copy(spill[g % 2][:, col0:col0 + 512], bank[:])
                            ins.then_inc(sm["evr"].h if ti == 3 else sm["rtc"].h, 1)
                            continue
                        e.wait_ge(sm["rt"].h, ro)
                        e.tensor_tensor(ropea[:], ropeu[(ro // 2) % 2][:],
                                        cs[:, tcol:tcol + 512],
                                        op=ALU.mult).then_inc(sm["rtc"].h, 1)
                        e.tensor_tensor(ropeb[:], ropet[(ro // 2) % 2][:],
                                        sn[:, tcol:tcol + 512],
                                        op=ALU.mult).then_inc(sm["rtc"].h, 1)
                        e.wait_ge(sm["rtc"].h, ent["rtc"][ti])
                        ins = e.tensor_add(spill[g % 2][:, col0:col0 + 512],
                                           ropea[:], ropeb[:])
                        if ti == 3:
                            ins.then_inc(sm["evr"].h, 1)

        @blk.tensor
        def _(e):
            if upto == "a1":
                return
            e.wait_ge(sm["set"].h, SET_AB)
            for h2 in range(2):
                if h2 == 1 and upto != "a2":
                    h0r = [z for z in b_sweeps if z["h2"] == 0 and z["wi"] != 2]
                    h0v = [z for z in b_sweeps if z["h2"] == 0 and z["wi"] == 2]
                    if h0r:
                        e.wait_ge(sm["evr"].h, h0r[-1]["ev"][1])
                    if h0v:
                        e.wait_ge(sm["evv"].h, h0v[-1]["ev"][1])
                for ent in [z for z in a2 if z["h2"] == h2]:
                    gi2 = ent["gi2"]
                    e.wait_ge(sm["ln"].h, ent["t_ln"])
                    for gtp in range(4):
                        gi = gi2 * 4 + gtp
                        if gi >= 2:
                            pent = a2[(gi - 2) // 4]
                            e.wait_ge(sm["tpf"].h, pent["t_tpf"][(gi - 2) % 4])
                        for j in range(4):
                            kc = 4 * gtp + j
                            ins = e.transpose(tp_ps[gi % 2][:, j * 128:(j + 1) * 128],
                                              lnb[gi2 % 2][:, kc * 128:(kc + 1) * 128],
                                              ident_t[:])
                            if j == 3:
                                ins.then_inc(sm["tp"].h, 1)
                last_a2 = [z for z in a2 if z["h2"] == h2][-1]
                e.wait_ge(sm["tpf"].h, last_a2["t_tpf"][3])
                for ent in ([] if upto == "a2" else [z for z in b_sweeps if z["h2"] == h2]):
                    wi, oc, g = ent["wi"], ent["oc"], ent["g"]
                    e.wait_ge(sm["w"].h, ent["t_w"])
                    if g >= 2:
                        sem, tgt = b_sweeps[g - 2]["ev"]
                        e.wait_ge(sm[sem].h, tgt)
                    for kci in range(KC):
                        e.ldweights(wblk[g % 4][:, kci * 128:(kci + 1) * 128])
                        for ti in range(4):
                            ins = e.matmul(pg[(g % 2) * 4 + ti][:],
                                           wblk[g % 4][:, kci * 128:(kci + 1) * 128],
                                           ln1c(kci, ti * 512, 512),
                                           start=(kci == 0), stop=(kci == KC - 1))
                            if kci == KC - 1 and ti == 3:
                                ins.then_inc(sm["qkvp"].h, 1)

    if upto in ("a1", "a2", "ln1", "qkv", "bq", "bq1", "bqm", "bqk", "bv"):
        return nc

    nc.sbuf_base = sbA
    kTb = [nc.alloc_sbuf_tensor(f"kTb{i}", [128, S], BF16) for i in range(2)]
    vTb = [nc.alloc_sbuf_tensor(f"vTb{i}", [128, S], BF16) for i in range(2)]
    qTb = [nc.alloc_sbuf_tensor(f"qTb{i}", [128, T], BF16) for i in range(2)]
    vh_t = nc.alloc_sbuf_tensor("vh_t", [128, S], BF16)
    expb = [nc.alloc_sbuf_tensor(f"expb{i}", [128, 1024], BF16) for i in range(8)]
    dacc = [nc.alloc_sbuf_tensor(f"dacc{i}", [128, 1024], BF16) for i in range(4)]
    dc1 = nc.alloc_sbuf_tensor("dc1", [128, 1024], BF16)
    dc2 = nc.alloc_sbuf_tensor("dc2", [128, 1024], BF16)
    den_sb = nc.alloc_sbuf_tensor("den_sb", [128, 1024], F32)
    den2_sb = nc.alloc_sbuf_tensor("den2_sb", [128, 1024], F32)
    recip_sb = nc.alloc_sbuf_tensor("recip_sb", [128, 1024], F32)
    aev = [nc.alloc_sbuf_tensor(f"aev{i}", [128, 1024], BF16) for i in range(2)]

    nc.psum_base = psA
    scps = [nc.alloc_psum_tensor(f"scps{i}", [128, 1024], F32) for i in range(2)]
    outps = [nc.alloc_psum_tensor(f"outps{i}", [128, 1024], F32) for i in range(2)]
    nc.psum_base = psA
    tpv_ps = [nc.alloc_psum_tensor(f"tpv{i}", [128, 512], BF16) for i in range(4)]
    nc.psum_base = psA + 8

    B_QKVP = sm["qkvp"].n
    B_EVR, B_EVV, B_SP = sm["evr"].n, sm["evv"].n, sm["sp"].n

    with nc.Block("C") as blk:
        @blk.sync
        def _(e):
            e.wait_ge(sm["qkvp"].h, B_QKVP)
            e.wait_ge(sm["evr"].h, B_EVR)
            e.wait_ge(sm["evv"].h, B_EVV)
            e.wait_ge(sm["sp"].h, B_SP)
            def spill_aev(hq):
                q2 = chq(hq)
                e.wait_ge(sm["oev"].h, q2["t_oev"])
                e.dma_start(out=attn_s[q2["h"], :,
                                       q2["qb"] * 1024:(q2["qb"] + 1) * 1024],
                            in_=aev[hq % 2][:]).then_inc(sm["aev"].h, 16)
            for q in c_hq:
                if q["qb"] == 1:
                    continue
                h = q["h"]
                if h >= 2:
                    e.wait_ge(sm["hd"].h, chq((h - 2) * 2 + 1)["t_hd"])
                e.dma_start(out=kTb[h % 2][:], in_=kT_s[h, :, :]).then_inc(sm["kvq"].h, 16)
                e.dma_start(out=vTb[h % 2][:], in_=vT_s[h, :, :]).then_inc(sm["kvq"].h, 16)
                e.dma_start(out=qTb[h % 2][:], in_=qT_s[h, :, :]).then_inc(sm["kvq"].h, 16)
                if h >= 1:
                    spill_aev((h - 1) * 2)
                    spill_aev((h - 1) * 2 + 1)
            spill_aev((NHC - 1) * 2)
            spill_aev((NHC - 1) * 2 + 1)
            if upto in ("attn", "c1", "c2", "c4"):
                e.wait_ge(sm["aev"].h, sm["aev"].n)

        @blk.tensor
        def _(e):
            e.wait_ge(sm["evr"].h, B_EVR)
            e.wait_ge(sm["evv"].h, B_EVV)
            for q in c_hq:
                h, qb, hq = q["h"], q["qb"], q["hq"]
                if qb == 0:
                    e.wait_ge(sm["kvq"].h, q["t_kvq"])
                    if h > 0:
                        e.wait_ge(sm["exp"].h, chq(h * 2 - 1)["t_exp"][31])
                    for gj in range(8):
                        if gj >= 2:
                            e.wait_ge(sm["tpvf"].h, q["t_tpvf"][gj - 2])
                        for j in range(4):
                            kc = gj * 4 + j
                            ins = e.transpose(tpv_ps[gj % 2][:, j * 128:(j + 1) * 128],
                                              vTb[h % 2][:, kc * 128:(kc + 1) * 128],
                                              ident_t[:])
                            if j == 3:
                                ins.then_inc(sm["tpv"].h, 1)
                    e.wait_ge(sm["tpvf"].h, q["t_tpvf"][7])
                def emit_scores(kc):
                    gexp = hq * 32 + kc
                    if qb == 0 and kc < 2:
                        pass
                    elif gexp >= 2:
                        pq, pkc = divmod(gexp - 2, 32)
                        e.wait_ge(sm["exp"].h, chq(pq)["t_exp"][pkc])
                    e.ldweights(kTb[h % 2][:, kc * 128:(kc + 1) * 128])
                    for half in range(2):
                        ins = e.matmul(scps[kc % 2][:, half * 512:(half + 1) * 512],
                                       kTb[h % 2][:, kc * 128:(kc + 1) * 128],
                                       qTb[h % 2][:, qb * 1024 + half * 512:
                                                   qb * 1024 + (half + 1) * 512],
                                       start=True, stop=True)
                        if half == 1:
                            ins.then_inc(sm["sc"].h, 1)

                def emit_v(kc):
                    e.wait_ge(sm["exp"].h, q["t_exp"][kc])
                    if kc == 0 and hq >= 2:
                        e.wait_ge(sm["oev"].h, chq(hq - 2)["t_oev"])
                    if kc == 0 and hq >= 1:
                        e.wait_ge(sm["den"].h, chq(hq - 1)["t_den"] - 1)
                    e.ldweights(vh_t[:, kc * 128:(kc + 1) * 128])
                    for half in range(2):
                        ins = e.matmul(outps[hq % 2][:, half * 512:(half + 1) * 512],
                                       vh_t[:, kc * 128:(kc + 1) * 128],
                                       expb[kc % 8][:, half * 512:(half + 1) * 512],
                                       start=(kc == 0), stop=(kc == 31))
                        if half == 1:
                            ins.then_inc(sm["vm"].h, 1)

                for kc in range(32):
                    emit_scores(kc)
                    if kc >= 1:
                        emit_v(kc - 1)
                emit_v(31)
                e.wait_ge(sm["dng"].h, q["t_dncmb"])
                if hq >= 1:
                    e.wait_ge(sm["den"].h, chq(hq - 1)["t_den"])
                for half in range(2):
                    ins = e.matmul(outps[(hq + 1) % 2][:, half * 512:(half + 1) * 512],
                                   ones128_t[:],
                                   den_sb[:, half * 512:(half + 1) * 512],
                                   start=True, stop=True)
                    if half == 1:
                        ins.then_inc(sm["denb"].h, 1)
                if qb == 1:
                    e.nop().then_inc(sm["hd"].h, 1)

        @blk.scalar
        def _(e):
            e.wait_ge(sm["setc"].h, 2)
            for q in c_hq:
                for kc in range(32):
                    gexp = q["hq"] * 32 + kc
                    e.wait_ge(sm["sc"].h, q["t_sc"][kc])
                    if gexp >= 8:
                        pq, pkc = divmod(gexp - 8, 32)
                        e.wait_ge(sm["vm"].h, chq(pq)["t_vm"][pkc])
                        e.wait_ge(sm["dng"].h,
                                  chq(pq)["t_dng"][pkc // 4] - 3 + pkc % 4)
                    e.activation(expb[kc % 8][:], scps[kc % 2][:], AF.Exp,
                                 bias=consts_t[:, 0:1]).then_inc(sm["exp"].h, 1)

        @blk.vector
        def _(e):
            for q in c_hq:
                h, qb, hq = q["h"], q["qb"], q["hq"]
                if qb == 0:
                    for gj in range(8):
                        e.wait_ge(sm["tpv"].h, q["t_tpv"][gj])
                        e.tensor_copy(vh_t[:, gj * 512:(gj + 1) * 512],
                                      tpv_ps[gj % 2][:]).then_inc(sm["tpvf"].h, 1)
                for gd in range(8):
                    for bidx in range(4):
                        kc = 4 * gd + bidx
                        e.wait_ge(sm["exp"].h, q["t_exp"][kc])
                        if gd == 0:
                            ins = e.tensor_copy(dacc[bidx][:], expb[kc % 8][:])
                        else:
                            ins = e.tensor_add(dacc[bidx][:], dacc[bidx][:],
                                               expb[kc % 8][:])
                        ins.then_inc(sm["dng"].h, 1)
                e.wait_ge(sm["dng"].h, q["t_dng"][7])
                e.tensor_add(dc1[:], dacc[0][:], dacc[1][:]).then_inc(sm["dng"].h, 1)
                e.tensor_add(dc2[:], dacc[2][:], dacc[3][:]).then_inc(sm["dng"].h, 1)
                e.wait_ge(sm["dng"].h, q["t_dncmb"] - 1)
                e.tensor_add(den_sb[:], dc1[:], dc2[:]).then_inc(sm["dng"].h, 1)
                e.wait_ge(sm["denb"].h, q["t_denb"])
                e.tensor_copy(den2_sb[:], outps[(hq + 1) % 2][:]).then_inc(sm["den"].h, 1)
                e.wait_ge(sm["den"].h, q["t_den"] - 1)
                e.reciprocal(recip_sb[:], den2_sb[:]).then_inc(sm["den"].h, 1)
                e.wait_ge(sm["den"].h, q["t_den"])
                e.wait_ge(sm["vm"].h, q["t_vm"][31])
                if hq >= 2:
                    e.wait_ge(sm["aev"].h, chq(hq - 2)["t_aev"])
                e.tensor_tensor(aev[hq % 2][:], outps[hq % 2][:], recip_sb[:],
                                op=ALU.mult).then_inc(sm["oev"].h, 1)

    if upto in ("attn", "c1", "c2", "c4"):
        return nc

    sbC = nc.sbuf_base
    WOA = 10
    woT_a = nc.alloc_sbuf_tensor("woT_a", [128, WOA * H], BF16)
    nc.sbuf_base = sbA
    woT_b = nc.alloc_sbuf_tensor("woT_b", [128, (OC - WOA) * H], BF16)
    attnD = [nc.alloc_sbuf_tensor(f"attnD{i}", [128, OC * 128], BF16) for i in range(2)]
    x2b = [nc.alloc_sbuf_tensor(f"x2b{i}", [128, H], F32) for i in range(2)]
    x2rb = [nc.alloc_sbuf_tensor(f"x2rb{i}", [128, H], F32) for i in range(2)]

    def woT_ap(cc, c0, n):
        if cc < WOA:
            return woT_a[:, cc * H + c0: cc * H + c0 + n]
        return woT_b[:, (cc - WOA) * H + c0: (cc - WOA) * H + c0 + n]

    nc.psum_base = psA
    yps = [nc.alloc_psum_tensor(f"yps{i}", [128, 512], F32) for i in range(4)]
    tp2_ps = [nc.alloc_psum_tensor(f"tp2_{i}", [128, 512], BF16) for i in range(2)]

    C_OEV, C_AEV, C_HD = sm["oev"].n, sm["aev"].n, sm["hd"].n

    with nc.Block("D") as blk:
        @blk.sync
        def _(e):
            def spill_x2(qc):
                e.wait_ge(sm["sq2"].h, d_qc[qc]["t_sq2"])
                e.dma_start(out=x2_s[qc, :, :],
                            in_=x2b[qc % 2][:]).then_inc(sm["spx2"].h, 16)
            for ent in d_qc:
                qc = ent["qc"]
                if qc >= 2:
                    e.wait_ge(sm["x2"].h, d_qc[qc - 2]["t_x2"])
                e.dma_start(out=xbuf[qc % 2][:],
                            in_=x_d[qc * 128:(qc + 1) * 128, :]).then_inc(sm["xr"].h, 16)
                if qc >= 1:
                    spill_x2(qc - 1)
            spill_x2(NQ - 1)
            for ent in d2_qc:
                qc = ent["qc"]
                e.wait_ge(sm["tpf2"].h, ent["t_tpf2"][3])
                dst = bass.AP(ln2T_s.tensor, ln2T_s.offset + qc * 128,
                              [[T, 128], [128 * T, KC], [1, 128]])
                e.dma_start(out=dst,
                            in_=spill[qc % 2][:, 0:KC * 128].rearrange(
                                "p (k c) -> p k c", k=KC)).then_inc(sm["spl2"].h, 16)

        @blk.gpsimd
        def _(e):
            e.wait_ge(sm["evr"].h, B_EVR)
            e.wait_ge(sm["evv"].h, B_EVV)
            for cc in range(OC):
                if cc == WOA:
                    e.wait_ge(sm["oev"].h, C_OEV)
                    e.wait_ge(sm["aev"].h, C_AEV)
                e.dma_start(
                    out=woT_ap(cc, 0, H).rearrange("p (o c) -> p o c", o=4),
                    in_=w_d["wot"][cc * 128:(cc + 1) * 128, :].rearrange(
                        "p (o c) -> p o c", c=512),
                ).then_inc(sm["wo"].h, 16)
            e.wait_ge(sm["oev"].h, C_OEV)
            e.wait_ge(sm["aev"].h, C_AEV)
            for ent in d_qc:
                qc = ent["qc"]
                if qc >= 2:
                    e.wait_ge(sm["y"].h, d_qc[qc - 2]["t_y"])
                src = bass.AP(attn_s.tensor, attn_s.offset + qc * 128,
                              [[T, 128], [128 * T, NH], [1, 128]])
                e.dma_start(out=attnD[qc % 2][:].rearrange("p (k c) -> p k c", k=NH),
                            in_=src).then_inc(sm["attnl"].h, 16)
            for ent in d2_qc:
                qc = ent["qc"]
                if qc >= 2:
                    e.wait_ge(sm["ln2"].h, d2_qc[qc - 2]["t_ln2"])
                e.wait_ge(sm["spx2"].h, d_qc[qc]["t_spx2"])
                e.dma_start(out=x2rb[qc % 2][:],
                            in_=x2_s[qc, :, :]).then_inc(sm["x2r"].h, 16)

        @blk.tensor
        def _(e):
            e.wait_ge(sm["oev"].h, C_OEV)
            for ent in d_qc:
                qc = ent["qc"]
                e.wait_ge(sm["attnl"].h, ent["t_attnl"])
                if qc >= 1:
                    e.wait_ge(sm["x2"].h, d_qc[qc - 1]["t_x2"])
                for cc in range(OC):
                    if qc == 0:
                        e.wait_ge(sm["wo"].h, t_wo[cc])
                    e.ldweights(attnD[qc % 2][:, cc * 128:(cc + 1) * 128])
                    for ot in range(4):
                        ins = e.matmul(yps[ot][:],
                                       attnD[qc % 2][:, cc * 128:(cc + 1) * 128],
                                       woT_ap(cc, ot * 512, 512),
                                       start=(cc == 0), stop=(cc == OC - 1))
                        if cc == OC - 1 and ot == 3:
                            ins.then_inc(sm["y"].h, 1)
            for ent in d2_qc:
                qc = ent["qc"]
                e.wait_ge(sm["ln2"].h, ent["t_ln2"])
                for g in range(4):
                    gi = qc * 4 + g
                    if gi >= 2:
                        pent = d2_qc[(gi - 2) // 4]
                        e.wait_ge(sm["tpf2"].h, pent["t_tpf2"][(gi - 2) % 4])
                    for j in range(4):
                        kc = 4 * g + j
                        ins = e.transpose(tp2_ps[gi % 2][:, j * 128:(j + 1) * 128],
                                          lnb[qc % 2][:, kc * 128:(kc + 1) * 128],
                                          ident_t[:])
                        if j == 3:
                            ins.then_inc(sm["tp2"].h, 1)

        @blk.scalar
        def _(e):
            for ent in d_qc:
                qc = ent["qc"]
                e.wait_ge(sm["x2"].h, ent["t_x2"])
                e.activation(sqtmp[:], x2b[qc % 2][:], AF.Square,
                             accum_out=stc(ST_SSUM, 32 + qc)).then_inc(sm["sq2"].h, 1)
            e.copy(str_(ST_TMP, 48, NQ), str_(ST_SSUM, 32, NQ)).then_inc(sm["sq2"].h, 1)
            e.wait_ge(sm["st21"].h, 1)
            e.activation(str_(ST_RSTD, 32, NQ), str_(ST_VAR, 32, NQ),
                         AF.Sqrt).then_inc(sm["st22"].h, 1)
            e.wait_ge(sm["st23"].h, 1)
            for ent in d2_qc:
                qc = ent["qc"]
                e.wait_ge(sm["x2r"].h, ent["t_x2r"])
                if qc >= 2:
                    e.wait_ge(sm["tp2"].h, d2_qc[qc - 2]["t_tp2"][3])
                e.activation(lnb[qc % 2][:], x2rb[qc % 2][:], AF.Identity,
                             bias=stc(ST_NMR, 32 + qc),
                             scale=stc(ST_RSTD, 32 + qc)).then_inc(sm["ln2"].h, 1)

        @blk.vector
        def _(e):
            for ent in d_qc:
                qc = ent["qc"]
                e.wait_ge(sm["y"].h, ent["t_y"])
                e.wait_ge(sm["xr"].h, ent["t_xr"])
                if qc >= 2:
                    e.wait_ge(sm["spx2"].h, d_qc[qc - 2]["t_spx2"])
                    e.wait_ge(sm["sq2"].h, d_qc[qc - 2]["t_sq2"])
                for ot in range(4):
                    ins = e.tensor_add(x2b[qc % 2][:, ot * 512:(ot + 1) * 512],
                                       yps[ot][:],
                                       xbuf[qc % 2][:, ot * 512:(ot + 1) * 512])
                    if ot == 3:
                        ins.then_inc(sm["x2"].h, 1)
                e.wait_ge(sm["x2"].h, ent["t_x2"])
                e.reduce_sum(stc(ST_XSUM, 32 + qc), x2b[qc % 2][:],
                             axis=AX.X).then_inc(sm["vs2"].h, 1)
            e.wait_ge(sm["sq2"].h, T_SQ2_FENCE)
            e.wait_ge(sm["vs2"].h, d_qc[-1]["t_vs2"])
            stp2 = [d_qc[-1]["t_vs2"]]
            def step2(ins):
                stp2[0] += 1
                ins.then_inc(sm["vs2"].h, 1)
                e.wait_ge(sm["vs2"].h, stp2[0])
            step2(e.tensor_scalar_mul(str_(ST_MEAN, 32, NQ), str_(ST_XSUM, 32, NQ), 1.0 / H))
            step2(e.tensor_scalar_mul(str_(ST_EX2, 32, NQ), str_(ST_SSUM, 32, NQ), 1.0 / H))
            step2(e.tensor_mul(str_(ST_TMP, 32, NQ), str_(ST_MEAN, 32, NQ), str_(ST_MEAN, 32, NQ)))
            step2(e.tensor_sub(str_(ST_VAR, 32, NQ), str_(ST_EX2, 32, NQ), str_(ST_TMP, 32, NQ)))
            step2(e.tensor_scalar_add(str_(ST_VAR, 32, NQ), str_(ST_VAR, 32, NQ), EPS))
            step2(e.reciprocal(str_(ST_VAR, 32, NQ), str_(ST_VAR, 32, NQ)))
            e.tensor_copy(str_(ST_SSUM, 48, NQ), str_(ST_VAR, 32, NQ)).then_inc(sm["st21"].h, 1)
            e.wait_ge(sm["st22"].h, 1)
            step2(e.scalar_tensor_tensor(str_(ST_NMR, 32, NQ), str_(ST_MEAN, 32, NQ), -1.0,
                                         str_(ST_RSTD, 32, NQ), op0=ALU.mult, op1=ALU.mult))
            e.tensor_copy(str_(ST_SSUM, 48, NQ), str_(ST_NMR, 32, NQ)).then_inc(sm["st23"].h, 1)
            for ent in d2_qc:
                qc = ent["qc"]
                for g in range(4):
                    gi = qc * 4 + g
                    e.wait_ge(sm["tp2"].h, ent["t_tp2"][g])
                    if qc >= 2 and g == 0:
                        e.wait_ge(sm["spl2"].h, d2_qc[qc - 2]["t_spl2"])
                    e.tensor_copy(spill[qc % 2][:, g * 512:(g + 1) * 512],
                                  tp2_ps[gi % 2][:]).then_inc(sm["tpf2"].h, 1)

    if upto == "wo":
        with nc.Block("Dend") as blk2:
            @blk2.sync
            def _(e):
                e.wait_ge(sm["spl2"].h, sm["spl2"].n)
                e.wait_ge(sm["spx2"].h, sm["spx2"].n)
        return nc

    nc.sbuf_base = sbA
    ln2q = [nc.alloc_sbuf_tensor(f"ln2q{i}", [128, KC * 512], BF16)
            for i in range(2)]
    h_sb = nc.alloc_sbuf_tensor("h_sb", [128, FC * 512], BF16)
    o0_sb = nc.alloc_sbuf_tensor("o0_sb", [128, OC * 512], BF16)
    mask_u8 = nc.alloc_sbuf_tensor("mask_u8", [128, 512], U8)
    mrow_sb = nc.alloc_sbuf_tensor("mrow_sb", [1, 512], BF16)
    gcp_sb = nc.alloc_sbuf_tensor("gcp_sb", [1, 512], F32)
    s1tmp = [nc.alloc_sbuf_tensor(f"s1tmp{i}", [128, 512], BF16) for i in range(2)]
    tmp0 = nc.alloc_sbuf_tensor("tmp0", [128, 512], BF16)
    finalb = [nc.alloc_sbuf_tensor(f"finalb{i}", [128, H], F32) for i in range(2)]

    nc.psum_base = psA
    hps = [nc.alloc_psum_tensor(f"hps{i}", [128, 512], F32) for i in range(2)]
    ops = [nc.alloc_psum_tensor(f"ops{i}", [128, 512], F32) for i in range(2)]
    psMid = nc.psum_base
    nc.psum_base = psA
    gps = nc.alloc_psum_tensor("gps", [1, 512], F32)
    mskps = nc.alloc_psum_tensor("mskps", [128, 512], F32)
    nc.psum_base = psMid
    psf = [nc.alloc_psum_tensor(f"psf{i}", [128, 2048], BF16) for i in range(2)]

    D_Y, D_SPL2, D_LN2 = sm["y"].n, sm["spl2"].n, sm["ln2"].n
    wE = {0: "w0at", 1: "w0bt", 2: "w1at", 3: "w1bt"}

    def eprev(bb, back):
        i = bb["stream_i"] - back
        return e_blocks[i] if i >= 0 else None

    with nc.Block("E") as blk:
        @blk.sync
        def _(e):
            e.wait_ge(sm["y"].h, D_Y)
            e.wait_ge(sm["spl2"].h, D_SPL2)
            def load_ln2q(qq):
                if qq >= 2:
                    e.wait_ge(sm["hp"].h, e_qq[qq - 2]["ph"][2]["blk"][FC - 1]["t_hp"])
                    e.wait_ge(sm["gate"].h, e_qq[qq - 2]["t_gate"])
                for kc in range(KC):
                    e.dma_start(out=ln2q[qq % 2][:, kc * 512:(kc + 1) * 512],
                                in_=ln2T_s[kc, :, qq * 512:(qq + 1) * 512]
                                ).then_inc(sm["lnq"].h, 16)
            load_ln2q(0)
            load_ln2q(1)
            for ent in e_qq:
                qq = ent["qq"]
                if qq + 2 < 4:
                    load_ln2q(qq + 2)
                for f in ent["fin"]:
                    qcB = f["qcB"]
                    e.wait_ge(sm["fin"].h, f["t_fin"])
                    e.dma_start(out=out_d[qq * 512 + qcB * 128:
                                          qq * 512 + (qcB + 1) * 128, :],
                                in_=finalb[qcB % 2][:]).then_inc(sm["outd"].h, 16)
            e.wait_ge(sm["outd"].h, sm["outd"].n)

        @blk.gpsimd
        def _(e):
            e.wait_ge(sm["y"].h, D_Y)
            e.wait_ge(sm["ln2"].h, D_LN2)
            for ent in e_qq:
                qq = ent["qq"]
                for p in ent["ph"]:
                    pi = p["pi"]
                    wd = w_d[wE[pi]]
                    nch = KC if pi in (0, 2) else FC
                    for j, bb in enumerate(p["blk"]):
                        pb = eprev(bb, 4)
                        if pb is not None:
                            e.wait_ge(sm["hp"].h if "t_hp" in pb else sm["op"].h,
                                      pb.get("t_hp") or pb.get("t_op"))
                        e.dma_start(
                            out=wblk[bb["stream_i"] % 4][:, 0:nch * 128].rearrange(
                                "p (k c) -> p k c", k=nch),
                            in_=wd[:, j * 128:(j + 1) * 128].rearrange(
                                "(k p) c -> p k c", p=128),
                        ).then_inc(sm["w2"].h, 16)
                for f in ent["fin"]:
                    qcB, gqc = f["qcB"], f["gqc"]
                    if gqc >= 2:
                        pf = e_qq[(gqc - 2) // 4]["fin"][(gqc - 2) % 4]
                        e.wait_ge(sm["fin"].h, pf["t_fin"])
                    e.dma_start(out=xbuf[qcB % 2][:],
                                in_=x2_s[gqc, :, :]).then_inc(sm["xe"].h, 16)

        @blk.tensor
        def _(e):
            e.wait_ge(sm["tpf2"].h, sm["tpf2"].n)
            for ent in e_qq:
                qq = ent["qq"]
                e.wait_ge(sm["lnq"].h, ent["t_lnq"])
                e.wait_ge(sm["setc"].h, 2)
                if qq >= 1:
                    e.wait_ge(sm["hev"].h, e_qq[qq - 1]["ph"][2]["blk"][FC - 1]["t_hev"])
                    e.wait_ge(sm["mskf"].h, e_qq[qq - 1]["t_mskf"])
                for kc in range(KC):
                    ins = e.matmul(gps[0:1, :], wgd_t[:, kc:kc + 1],
                                   ln2q[qq % 2][:, kc * 512:(kc + 1) * 512],
                                   start=(kc == 0), stop=(kc == KC - 1))
                    if kc == KC - 1:
                        ins.then_inc(sm["gate"].h, 1)
                e.wait_ge(sm["msk"].h, ent["t_msk"])
                e.matmul(mskps[:], onesrow_t[:], mrow_sb[0:1, :],
                         start=True, stop=True).then_inc(sm["mskp"].h, 1)
                for p in ent["ph"]:
                    pi = p["pi"]
                    hpass = pi in (0, 2)
                    ncha = KC if hpass else FC
                    for j, bb in enumerate(p["blk"]):
                        e.wait_ge(sm["w2"].h, bb["t_w2"])
                        fam_prev = None
                        bi = bb["stream_i"] - 1
                        cnt = 0
                        while bi >= 0:
                            cb = e_blocks[bi]
                            if (cb["pi"] in (0, 2)) == hpass:
                                cnt += 1
                                if cnt == 2:
                                    fam_prev = cb
                                    break
                            bi -= 1
                        if fam_prev is not None:
                            if "t_hev" in fam_prev:
                                e.wait_ge(sm["hev"].h, fam_prev["t_hev"])
                            elif "t_oevB" in fam_prev:
                                e.wait_ge(sm["oevB"].h, fam_prev["t_oevB"])
                            else:
                                e.wait_ge(sm["s1"].h, fam_prev["t_s1"])
                        if pi == 0 and j < 2:
                            e.wait_ge(sm["mskf"].h, ent["t_mskf"])
                        if pi in (1, 3) and j == 0:
                            e.wait_ge(sm["hev"].h, ent["ph"][pi - 1]["blk"][FC - 1]["t_hev"])
                        if pi == 2 and j == 0:
                            e.wait_ge(sm["op"].h, ent["ph"][1]["blk"][OC - 1]["t_op"])
                        if pi == 3 and j == 0 and qq >= 1:
                            pass
                        ps = (hps if hpass else ops)[j % 2]
                        rhsrc = ln2q[qq % 2] if hpass else h_sb
                        for kci in range(ncha):
                            e.ldweights(wblk[bb["stream_i"] % 4][:,
                                        kci * 128:(kci + 1) * 128])
                            ins = e.matmul(ps[:],
                                           wblk[bb["stream_i"] % 4][:,
                                           kci * 128:(kci + 1) * 128],
                                           rhsrc[:, kci * 512:(kci + 1) * 512],
                                           start=(kci == 0), stop=(kci == ncha - 1))
                            if kci == ncha - 1:
                                ins.then_inc(sm["hp"].h if hpass else sm["op"].h, 1)
                for f in ent["fin"]:
                    qcB, gqc = f["qcB"], f["gqc"]
                    if qcB == 0:
                        e.wait_ge(sm["bl"].h, ent["ph"][3]["blk"][OC - 1]["t_bl"])
                        e.wait_ge(sm["s1"].h, ent["ph"][3]["blk"][OC - 1]["t_s1"])
                    if gqc >= 2:
                        pf = e_qq[(gqc - 2) // 4]["fin"][(gqc - 2) % 4]
                        e.wait_ge(sm["fin"].h, pf["t_fin"])
                    for ocj in range(OC):
                        ins = e.transpose(psf[gqc % 2][:, ocj * 128:(ocj + 1) * 128],
                                          o0_sb[:, ocj * 512 + qcB * 128:
                                                ocj * 512 + (qcB + 1) * 128],
                                          ident_t[:])
                        if ocj == OC - 1:
                            ins.then_inc(sm["tpe"].h, 1)

        @blk.scalar
        def _(e):
            for ent in e_qq:
                for p in ent["ph"]:
                    pi = p["pi"]
                    for j, bb in enumerate(p["blk"]):
                        if pi in (0, 2):
                            e.wait_ge(sm["hp"].h, bb["t_hp"])
                            func = AF.Gelu if pi == 0 else AF.Relu
                            bias = bt["b0a"] if pi == 0 else bt["b1a"]
                            e.activation(h_sb[:, j * 512:(j + 1) * 512],
                                         hps[j % 2][:], func,
                                         bias=bias[:, j:j + 1]).then_inc(sm["hev"].h, 1)
                        elif pi == 1:
                            e.wait_ge(sm["op"].h, bb["t_op"])
                            e.activation(o0_sb[:, j * 512:(j + 1) * 512],
                                         ops[j % 2][:], AF.Identity,
                                         bias=bt["b0b"][:, j:j + 1]).then_inc(sm["oevB"].h, 1)
                        else:
                            e.wait_ge(sm["op"].h, bb["t_op"])
                            if j >= 2:
                                e.wait_ge(sm["bl"].h, p["blk"][j - 2]["t_bl"])
                            e.activation(s1tmp[j % 2][:], ops[j % 2][:], AF.Identity,
                                         bias=bt["b1b"][:, j:j + 1]).then_inc(sm["s1"].h, 1)

        @blk.vector
        def _(e):
            for ent in e_qq:
                qq = ent["qq"]
                e.wait_ge(sm["gate"].h, ent["t_gate"])
                e.tensor_copy(gcp_sb[:], gps[0:1, :]).then_inc(sm["msk"].h, 1)
                e.wait_ge(sm["msk"].h, ent["t_mskcp"])
                e.tensor_single_scalar(mrow_sb[:], gcp_sb[:], 0.0,
                                       op=ALU.is_ge).then_inc(sm["msk"].h, 1)
                e.wait_ge(sm["mskp"].h, ent["t_mskp"])
                e.tensor_copy(mask_u8[:], mskps[:]).then_inc(sm["mskf"].h, 1)
                p3 = ent["ph"][3]
                for j, bb in enumerate(p3["blk"]):
                    e.wait_ge(sm["s1"].h, bb["t_s1"])
                    e.tensor_copy(tmp0[:], o0_sb[:, j * 512:(j + 1) * 512]
                                  ).then_inc(sm["bl"].h, 1)
                    e.wait_ge(sm["bl"].h, bb["t_bl"] - 1)
                    e.select(o0_sb[:, j * 512:(j + 1) * 512], mask_u8[:],
                             tmp0[:], s1tmp[j % 2][:]).then_inc(sm["bl"].h, 1)
                for f in ent["fin"]:
                    qcB, gqc = f["qcB"], f["gqc"]
                    e.wait_ge(sm["tpe"].h, f["t_tpe"])
                    e.wait_ge(sm["xe"].h, f["t_xe"])
                    if gqc >= 2:
                        pf = e_qq[(gqc - 2) // 4]["fin"][(gqc - 2) % 4]
                        e.wait_ge(sm["outd"].h, pf["t_out"])
                    for ot in range(4):
                        ins = e.tensor_add(finalb[qcB % 2][:, ot * 512:(ot + 1) * 512],
                                           psf[gqc % 2][:, ot * 512:(ot + 1) * 512],
                                           xbuf[qcB % 2][:, ot * 512:(ot + 1) * 512])
                        if ot == 3:
                            ins.then_inc(sm["fin"].h, 1)

    return nc


_CACHE = {}


def _tables():
    inv = (1.0 / (1000000.0 ** (np.arange(0, HD, 2, dtype=np.float32) / HD))
           ).astype(np.float32)
    pos = np.arange(S, dtype=np.float32)
    fr = np.outer(pos, inv).astype(np.float32)
    cosT = np.concatenate([np.cos(fr).T, np.cos(fr).T], 0)
    sinT = np.concatenate([np.sin(fr).T, np.sin(fr).T], 0)
    return cosT, sinT


def make_inmaps(inputs):
    X = np.asarray(inputs["L0_attn_hidden_states"])
    cosT, sinT = _tables()
    sc = np.float32(1.0 / np.sqrt(HD))
    shared = {
        "wqt": np.ascontiguousarray(np.asarray(inputs["L0_attn_Wq"]).T),
        "wkt": np.ascontiguousarray(np.asarray(inputs["L0_attn_Wk"]).T),
        "wvt": np.ascontiguousarray(np.asarray(inputs["L0_attn_Wv"]).T),
        "wot": np.ascontiguousarray(np.asarray(inputs["L0_attn_Wo"]).T),
        "w0at": np.ascontiguousarray(np.asarray(inputs["L0_ffn_W0a"]).T),
        "w0bt": np.ascontiguousarray(np.asarray(inputs["L0_ffn_W0b"]).T),
        "w1at": np.ascontiguousarray(np.asarray(inputs["L0_ffn_W1a"]).T),
        "w1bt": np.ascontiguousarray(np.asarray(inputs["L0_ffn_W1b"]).T),
        "bq": np.asarray(inputs["L0_attn_bq"]),
        "bk": np.asarray(inputs["L0_attn_bk"]),
        "bv": np.asarray(inputs["L0_attn_bv"]),
        "wgt": np.ascontiguousarray(np.asarray(inputs["L0_ffn_Wg"]).T),
        "b0a": np.asarray(inputs["L0_ffn_b0a"]),
        "b0b": np.asarray(inputs["L0_ffn_b0b"]),
        "b1a": np.asarray(inputs["L0_ffn_b1a"]),
        "b1b": np.asarray(inputs["L0_ffn_b1b"]),
        "ident": np.eye(128, dtype=ml_dtypes.bfloat16),
        "ones128": np.ones((128, 128), np.float32),
        "onesrow": np.ones((1, 128), ml_dtypes.bfloat16),
    }
    in_maps = []
    for c in range(8):
        b, half = c // 2, c % 2
        roll = -half * T
        in_maps.append(dict(shared, **{
            "x": np.ascontiguousarray(np.roll(X[b], roll, axis=0)),
            "cosq": (cosT[:, half * T:(half + 1) * T] * sc).astype(ml_dtypes.bfloat16),
            "sinq": (sinT[:, half * T:(half + 1) * T] * sc).astype(ml_dtypes.bfloat16),
            "cosk": np.ascontiguousarray(np.roll(cosT, roll, axis=1)
                                         ).astype(ml_dtypes.bfloat16),
            "sink": np.ascontiguousarray(np.roll(sinT, roll, axis=1)
                                         ).astype(ml_dtypes.bfloat16),
        }))
    return in_maps


def kernel(**inputs):
    if "full" not in _CACHE:
        _CACHE["full"] = build("full")
    nc = _CACHE["full"]
    in_maps = make_inmaps(inputs)
    res = run_bass_kernel_spmd(nc, in_maps, core_ids=list(range(8)))
    OUT = np.empty((B, S, H), np.float32)
    for c in range(8):
        b, half = c // 2, c % 2
        OUT[b, half * T:(half + 1) * T, :] = res.results[c]["out"]
    return OUT


# version 06
# speedup vs baseline: 1.1299x; 1.0098x over previous
import numpy as np
import ml_dtypes
import concourse.bass as bass
import concourse.mybir as mybir
from concourse.bass_utils import run_bass_kernel_spmd

F32 = mybir.dt.float32
BF16 = mybir.dt.bfloat16
U8 = mybir.dt.uint8
AF = mybir.ActivationFunctionType
AX = mybir.AxisListType
ALU = mybir.AluOpType

B, S, H, NH, HD, FF = 4, 4096, 2048, 16, 128, 4096
T = S // 2
KC, OC, FC = H // 128, H // 128, FF // 128
NT = S // 128
NQ = T // 128
NEGSTAB = -4.0
EPS = 1e-5


class SemT:
    def __init__(self, nc, name):
        self.h = nc.alloc_semaphore(name=name)
        self.n = 0

    def tag(self, k=1):
        self.n += k
        return self.n


def build(upto="full"):
    nc = bass.Bass("TRN2", target_bir_lowering=False, debug=False)
    dbg = upto != "full"

    x_d = nc.dram_tensor("x", [S, H], F32, kind="ExternalInput").ap()
    w_d = {nm: nc.dram_tensor(nm, shp, F32, kind="ExternalInput").ap()
           for nm, shp in [("wqt", [H, H]), ("wkt", [H, H]), ("wvt", [H, H]),
                           ("wot", [H, H]), ("w0at", [H, FF]), ("w0bt", [FF, H]),
                           ("w1at", [H, FF]), ("w1bt", [FF, H])]}
    b_d = {nm: nc.dram_tensor(nm, [n], F32, kind="ExternalInput").ap()
           for nm, n in [("bq", H), ("bk", H), ("bv", H), ("b0a", FF),
                         ("b0b", H), ("b1a", FF), ("b1b", H)]}
    wgt_d = nc.dram_tensor("wgt", [H, 2], F32, kind="ExternalInput").ap()
    cosq_d = nc.dram_tensor("cosq", [128, T], BF16, kind="ExternalInput").ap()
    sinq_d = nc.dram_tensor("sinq", [128, T], BF16, kind="ExternalInput").ap()
    cosk_d = nc.dram_tensor("cosk", [128, S], BF16, kind="ExternalInput").ap()
    sink_d = nc.dram_tensor("sink", [128, S], BF16, kind="ExternalInput").ap()
    ident_d = nc.dram_tensor("ident", [128, 128], BF16, kind="ExternalInput").ap()
    ones128_d = nc.dram_tensor("ones128", [128, 128], F32, kind="ExternalInput").ap()
    onesrow_d = nc.dram_tensor("onesrow", [1, 128], BF16, kind="ExternalInput").ap()
    out_d = nc.dram_tensor("out", [T, H], F32, kind="ExternalOutput").ap()

    def scr(name, shape, dtype, expose):
        kind = "ExternalOutput" if (dbg and expose) else "Internal"
        return nc.dram_tensor(name, shape, dtype, kind=kind).ap()

    qT_s = scr("qT_s", [NH, 128, T], BF16, upto in ("qkv", "c1", "c2", "c4", "fullx"))
    kT_s = scr("kT_s", [NH, 128, S], BF16, upto in ("qkv", "c1", "c2", "c4", "fullx"))
    vT_s = scr("vT_s", [NH, 128, S], BF16, upto in ("qkv", "c1", "c2", "c4", "fullx"))
    attn_s = scr("attn_s", [NH, 128, T], BF16, upto in ("attn", "c1", "c2", "c4", "fullx"))
    x2_s = scr("x2_s", [NQ, 128, H], F32, upto in ("wo", "fullx"))
    ln2T_s = scr("ln2T_s", [KC, 128, T], BF16, upto in ("wo", "fullx"))
    ln1_dbg = scr("ln1_dbg", [KC, 128, S], BF16, upto in ("ln1", "a2"))
    stats_dbg = scr("stats_dbg", [128, 8 * 64], F32, upto == "a2")

    A = nc.alloc_sbuf_tensor
    bt = {nm: A(nm + "_t", [128, OC], F32) for nm in ["bq", "bk", "bv", "b0b", "b1b"]}
    bt["b0a"] = A("b0a_t", [128, FC], F32)
    bt["b1a"] = A("b1a_t", [128, FC], F32)
    bqsw = A("bqsw", [128, OC], F32)
    bksw = A("bksw", [128, OC], F32)
    wg_t = A("wg_t", [128, 2 * KC], F32)
    wgd_t = A("wgd_t", [128, KC], BF16)
    ident_t = A("ident_t", [128, 128], BF16)
    ones128_t = A("ones128_t", [128, 128], F32)
    onesrow_t = A("onesrow_t", [1, 128], BF16)
    consts_t = A("consts_t", [128, 4], F32)
    stats_t = A("stats_t", [128, 8 * 64], F32)
    xbuf = [A(f"xbuf{i}", [128, H], F32) for i in range(2)]
    wblk = [A(f"wblk{i}", [128, FF], BF16) for i in range(4)]
    spill = [A(f"spill{i}", [128, T], BF16) for i in range(2)]
    lnb = [A(f"lnb{i}", [128, H], BF16) for i in range(2)]
    sqtmp = A("sqtmp", [128, H], BF16)

    ST_XSUM, ST_SSUM, ST_MEAN, ST_EX2, ST_VAR, ST_RSTD, ST_NMR, ST_TMP = range(8)

    def stc(kind, tile):
        return stats_t[:, kind * 64 + tile: kind * 64 + tile + 1]

    def str_(kind, t0, n):
        return stats_t[:, kind * 64 + t0: kind * 64 + t0 + n]

    SN = ["set", "seth", "setc",
          "x", "sq", "vs", "st1", "st2", "st3", "ln", "tp", "tpf",
          "w", "qkvp", "evr", "evv", "sp", "rt", "rtc",
          "kvq", "tpv", "tpvf", "sc", "exp", "vm", "dng", "denb", "den",
          "oev", "aev", "hd",
          "wo", "attnl", "y", "xr", "x2", "sq2", "vs2", "st21", "st22", "st23",
          "x2r", "ln2", "tp2", "tpf2", "spl2", "spx2",
          "lnq", "gate", "msk", "mskp", "mskf", "w2", "hp", "hev", "op", "oevB",
          "s1", "bl", "tpe", "xe", "fin", "outd", "dbg"]
    sm = {nm: SemT(nc, "s_" + nm) for nm in SN}

    a1 = []
    for tt in range(NT):
        a1.append(dict(tt=tt, t_x=sm["x"].tag(16), t_sq=sm["sq"].tag(),
                       t_vs=sm["vs"].tag()))
    T_SQ_FENCE = sm["sq"].tag()
    _ = [sm["vs"].tag() for _ in range(8)]
    a2 = []
    for h2 in range(2):
        for lt in range(16):
            e = dict(h2=h2, lt=lt, tt=h2 * 16 + lt, gi2=len(a2))
            e["t_x"] = sm["x"].tag(16)
            e["t_ln"] = sm["ln"].tag()
            e["t_tp"] = [sm["tp"].tag() for _ in range(4)]
            e["t_tpf"] = [sm["tpf"].tag() for _ in range(4)]
            a2.append(e)

    b_sweeps = []
    for h2 in range(2):
        wis = [0, 1, 2] if h2 == 0 else [1, 2]
        for wi in wis:
            for oc in range(OC):
                b_sweeps.append(dict(h2=h2, wi=wi, oc=oc))
    if upto == "bq":
        b_sweeps = [z for z in b_sweeps if z["wi"] == 0]
    elif upto in ("bq1", "bqm"):
        b_sweeps = [z for z in b_sweeps if z["wi"] == 0 and z["oc"] == 0]
    elif upto == "bqk":
        b_sweeps = [z for z in b_sweeps if z["wi"] != 2]
    elif upto == "bv":
        b_sweeps = [z for z in b_sweeps if z["wi"] == 2]
    for g, ent in enumerate(b_sweeps):
        ent["g"] = g
        ent["t_w"] = sm["w"].tag(16)
        ent["t_qkvp"] = sm["qkvp"].tag()
        if ent["wi"] in (0, 1):
            ent["ev"] = ("evr", sm["evr"].tag())
            ent["rt"] = [sm["rt"].tag(2) for _ in range(4)]
            ent["rtc"] = [sm["rtc"].tag(2) for _ in range(4)]
        else:
            ent["ev"] = ("evv", sm["evv"].tag())
        ent["t_sp"] = sm["sp"].tag(16)
    NB = len(b_sweeps)

    NHC = {"c1": 1, "c2": 2, "c4": 4}.get(upto, NH)
    c_hq = []
    for h in range(NHC):
        for qb in range(2):
            q = dict(h=h, qb=qb, hq=h * 2 + qb)
            if qb == 0:
                q["t_kvq"] = sm["kvq"].tag(48)
                q["t_tpv"] = [sm["tpv"].tag() for _ in range(8)]
                q["t_tpvf"] = [sm["tpvf"].tag() for _ in range(8)]
            q["t_sc"] = [sm["sc"].tag() for _ in range(32)]
            q["t_exp"] = [sm["exp"].tag() for _ in range(32)]
            q["t_vm"] = [sm["vm"].tag() for _ in range(32)]
            q["t_dng"] = [sm["dng"].tag(4) for _ in range(8)]
            _ = sm["dng"].tag(2)
            q["t_dncmb"] = sm["dng"].tag()
            q["t_denb"] = sm["denb"].tag()
            _ = sm["den"].tag()
            q["t_den"] = sm["den"].tag()
            q["t_oev"] = sm["oev"].tag()
            q["t_aev"] = sm["aev"].tag(16)
            if qb == 1:
                q["t_hd"] = sm["hd"].tag()
            c_hq.append(q)

    def chq(hq):
        return c_hq[hq]

    t_wo = [sm["wo"].tag(16) for _ in range(OC)]
    d_qc = []
    for qc in range(NQ):
        e = dict(qc=qc, t_attnl=sm["attnl"].tag(16), t_y=sm["y"].tag(),
                 t_xr=sm["xr"].tag(16), t_x2=sm["x2"].tag(),
                 t_sq2=sm["sq2"].tag(), t_vs2=sm["vs2"].tag(),
                 t_spx2=sm["spx2"].tag(16))
        d_qc.append(e)
    T_SQ2_FENCE = sm["sq2"].tag()
    _ = [sm["vs2"].tag() for _ in range(8)]
    d2_qc = []
    for qc in range(NQ):
        e = dict(qc=qc, t_x2r=sm["x2r"].tag(16), t_ln2=sm["ln2"].tag(),
                 t_tp2=[sm["tp2"].tag() for _ in range(4)],
                 t_tpf2=[sm["tpf2"].tag() for _ in range(4)],
                 t_spl2=sm["spl2"].tag(16))
        d2_qc.append(e)

    e_blocks = []
    e_qq = []
    for qq in range(4):
        e = dict(qq=qq)
        e["t_lnq"] = sm["lnq"].tag(16 * 16)
        e["t_gate"] = sm["gate"].tag()
        e["t_mskcp"] = sm["msk"].tag()
        e["t_msk"] = sm["msk"].tag()
        e["t_mskp"] = sm["mskp"].tag()
        e["t_mskf"] = sm["mskf"].tag()
        e["ph"] = []
        for pi, nblk in enumerate([FC, OC, FC, OC]):
            p = dict(pi=pi, blk=[])
            for j in range(nblk):
                bb = dict(qq=qq, pi=pi, j=j)
                bb["t_w2"] = sm["w2"].tag(16)
                if pi in (0, 2):
                    bb["t_hp"] = sm["hp"].tag()
                    bb["t_hev"] = sm["hev"].tag()
                else:
                    bb["t_op"] = sm["op"].tag()
                    if pi == 1:
                        bb["t_oevB"] = sm["oevB"].tag()
                    else:
                        bb["t_s1"] = sm["s1"].tag()
                        _ = sm["bl"].tag()
                        bb["t_bl"] = sm["bl"].tag()
                bb["stream_i"] = len(e_blocks)
                e_blocks.append(bb)
                p["blk"].append(bb)
            e["ph"].append(p)
        e["fin"] = []
        for qcB in range(4):
            f = dict(qcB=qcB, gqc=qq * 4 + qcB)
            f["t_tpe"] = sm["tpe"].tag()
            f["t_xe"] = sm["xe"].tag(16)
            f["t_fin"] = sm["fin"].tag()
            f["t_out"] = sm["outd"].tag(16)
            e["fin"].append(f)
        e_qq.append(e)

    with nc.Block("setup") as blk:
        @blk.sync
        def _(e):
            for dst, src in [(ident_t, ident_d), (ones128_t, ones128_d),
                             (onesrow_t, onesrow_d)]:
                e.dma_start(out=dst[:], in_=src[:]).then_inc(sm["seth"].h, 16)

        @blk.gpsimd
        def _(e):
            e.memset(consts_t[:, 0:1], NEGSTAB).then_inc(sm["setc"].h, 1)
            for nm in ["bq", "bk", "bv", "b0b", "b1b", "b0a", "b1a"]:
                e.dma_start(out=bt[nm][:], in_=b_d[nm].rearrange("(o p) -> p o", p=128),
                            allow_slow_non_contiguous=True).then_inc(sm["set"].h, 16)
            e.dma_start(out=wg_t[:].rearrange("p (k e) -> p k e", e=2),
                        in_=wgt_d.rearrange("(k p) e -> p k e", p=128),
                        allow_slow_non_contiguous=True).then_inc(sm["set"].h, 16)

        @blk.vector
        def _(e):
            e.wait_ge(sm["set"].h, 16 * 8)
            e.wait_ge(sm["seth"].h, 16 * 3)
            e.tensor_scalar_mul(bqsw[0:64, :], bt["bq"][64:128, :], -1.0)
            e.tensor_scalar_mul(bqsw[64:128, :], bt["bq"][0:64, :], 1.0)
            e.tensor_scalar_mul(bksw[0:64, :], bt["bk"][64:128, :], -1.0)
            e.tensor_scalar_mul(bksw[64:128, :], bt["bk"][0:64, :], 1.0)
            wg3 = wg_t[:].rearrange("p (k e) -> p k e", e=2)
            e.tensor_tensor(wgd_t[:], wg3[:, :, 0], wg3[:, :, 1],
                            op=ALU.subtract).then_inc(sm["setc"].h, 1)

    sm["setc"].n = 2
    SET_SW = 16 * 8
    SET_HW = 16 * 7

    if upto == "setup":
        with nc.Block("term") as blk:
            @blk.sync
            def _(e):
                e.wait_ge(sm["set"].h, 16 * 8)
                e.wait_ge(sm["seth"].h, 16 * 3)
                e.wait_ge(sm["setc"].h, 2)
        return nc

    sbA = nc.sbuf_base
    ln1T = nc.alloc_sbuf_tensor("ln1T", [128, KC * T], BF16)
    cosq_t = nc.alloc_sbuf_tensor("cosq_t", [128, T], BF16)
    sinq_t = nc.alloc_sbuf_tensor("sinq_t", [128, T], BF16)
    cosk_t = nc.alloc_sbuf_tensor("cosk_t", [128, S], BF16)
    sink_t = nc.alloc_sbuf_tensor("sink_t", [128, S], BF16)
    ropet = [nc.alloc_sbuf_tensor(f"ropet{i}", [128, 512], F32) for i in range(2)]
    ropeu = [nc.alloc_sbuf_tensor(f"ropeu{i}", [128, 512], F32) for i in range(2)]
    ropea = nc.alloc_sbuf_tensor("ropea", [128, 512], F32)
    ropeb = nc.alloc_sbuf_tensor("ropeb", [128, 512], F32)

    psA = nc.psum_base
    pg = [nc.alloc_psum_tensor(f"pg{i}", [128, 512], F32) for i in range(8)]
    nc.psum_base = psA
    tp_ps = [nc.alloc_psum_tensor(f"tp{i}", [128, 512], BF16) for i in range(2)]
    nc.psum_base = psA + 8

    def ln1c(kc, c0, n):
        return ln1T[:, kc * T + c0: kc * T + c0 + n]

    with nc.Block("AB") as blk:
        @blk.sync
        def _(e):
            for dst, src in [(cosq_t, cosq_d), (sinq_t, sinq_d),
                             (cosk_t, cosk_d), (sink_t, sink_d)]:
                e.dma_start(out=dst[:], in_=src[:]).then_inc(sm["seth"].h, 16)
            for ent in a1:
                tt = ent["tt"]
                if tt >= 2:
                    e.wait_ge(sm["sq"].h, a1[tt - 2]["t_sq"])
                    e.wait_ge(sm["vs"].h, a1[tt - 2]["t_vs"])
                e.dma_start(out=xbuf[tt % 2][:],
                            in_=x_d[tt * 128:(tt + 1) * 128, :]).then_inc(sm["x"].h, 16)
            if upto == "a1":
                e.wait_ge(sm["st3"].h, 1)
                e.wait_ge(sm["st2"].h, 1)
                return
            for h2 in range(2):
                for ent in [z for z in a2 if z["h2"] == h2]:
                    tt, gi2 = ent["tt"], ent["gi2"]
                    if gi2 >= 2:
                        e.wait_ge(sm["ln"].h, a2[gi2 - 2]["t_ln"])
                    else:
                        e.wait_ge(sm["sq"].h, a1[NT - 1]["t_sq"])
                        e.wait_ge(sm["vs"].h, a1[NT - 1]["t_vs"])
                    e.dma_start(out=xbuf[gi2 % 2][:],
                                in_=x_d[tt * 128:(tt + 1) * 128, :]).then_inc(sm["x"].h, 16)
                for ent in ([] if upto == "a2" else [z for z in b_sweeps if z["h2"] == h2]):
                    sem, tgt = ent["ev"]
                    e.wait_ge(sm[sem].h, tgt)
                    wi, oc = ent["wi"], ent["oc"]
                    dest = [qT_s, kT_s, vT_s][wi]
                    col0 = 0 if wi == 0 else h2 * T
                    e.dma_start(out=dest[oc, :, col0:col0 + T],
                                in_=spill[ent["g"] % 2][:]).then_inc(sm["sp"].h, 16)
            if upto in ("qkv", "bq", "bq1", "bqm", "bqk", "bv"):
                e.wait_ge(sm["sp"].h, sm["sp"].n)
            if upto in ("ln1", "a2"):
                e.wait_ge(sm["tpf"].h, a2[-1]["t_tpf"][3])
                if upto == "a2":
                    e.dma_start(out=stats_dbg[:, :], in_=stats_t[:]).then_inc(sm["dbg"].h, 16)
                for kc in range(KC):
                    e.dma_start(out=ln1_dbg[kc, :, T:S], in_=ln1c(kc, 0, T)
                                ).then_inc(sm["dbg"].h, 16)
                e.wait_ge(sm["dbg"].h, 16 * KC)

        @blk.gpsimd
        def _(e):
            if upto in ("a1", "a2"):
                return
            wts = {0: w_d["wqt"], 1: w_d["wkt"], 2: w_d["wvt"]}
            for ent in b_sweeps:
                g, wi, oc = ent["g"], ent["wi"], ent["oc"]
                if g >= 4:
                    e.wait_ge(sm["qkvp"].h, b_sweeps[g - 4]["t_qkvp"])
                e.dma_start(
                    out=wblk[g % 4][:, 0:KC * 128].rearrange("p (k c) -> p k c", k=KC),
                    in_=wts[wi][:, oc * 128:(oc + 1) * 128].rearrange(
                        "(k p) c -> p k c", p=128),
                ).then_inc(sm["w"].h, 16)

        @blk.scalar
        def _(e):
            for ent in a1:
                tt = ent["tt"]
                e.wait_ge(sm["x"].h, ent["t_x"])
                e.activation(sqtmp[:], xbuf[tt % 2][:], AF.Square,
                             accum_out=stc(ST_SSUM, tt)).then_inc(sm["sq"].h, 1)
            e.copy(str_(ST_TMP, 32, NT), str_(ST_SSUM, 0, NT)).then_inc(sm["sq"].h, 1)
            e.wait_ge(sm["st1"].h, 1)
            e.activation(str_(ST_RSTD, 0, NT), str_(ST_VAR, 0, NT),
                         AF.Sqrt).then_inc(sm["st2"].h, 1)
            if upto == "a1":
                return
            e.wait_ge(sm["st3"].h, 1)
            e.wait_ge(sm["setc"].h, 2)
            for h2 in range(2):
                for ent in [z for z in a2 if z["h2"] == h2]:
                    tt, gi2 = ent["tt"], ent["gi2"]
                    e.wait_ge(sm["x"].h, ent["t_x"])
                    if gi2 >= 2:
                        e.wait_ge(sm["tp"].h, a2[gi2 - 2]["t_tp"][3])
                    e.activation(lnb[gi2 % 2][:], xbuf[gi2 % 2][:], AF.Identity,
                                 bias=stc(ST_NMR, tt),
                                 scale=stc(ST_RSTD, tt)).then_inc(sm["ln"].h, 1)
                for ent in ([] if upto == "a2" else [z for z in b_sweeps if z["h2"] == h2]):
                    wi, oc, g = ent["wi"], ent["oc"], ent["g"]
                    e.wait_ge(sm["qkvp"].h, ent["t_qkvp"])
                    if g >= 2:
                        e.wait_ge(sm["sp"].h, b_sweeps[g - 2]["t_sp"])
                    if wi == 2:
                        for ti in range(4):
                            ins = e.activation(spill[g % 2][:, ti * 512:(ti + 1) * 512],
                                               pg[(g % 2) * 4 + ti][:], AF.Identity,
                                               bias=bt["bv"][:, oc:oc + 1])
                            if ti == 3:
                                ins.then_inc(sm["evv"].h, 1)
                    elif upto != "bqm":
                        bias_t = bt["bq"] if wi == 0 else bt["bk"]
                        for ti in range(4):
                            ro = ent["rt"][ti]
                            if ro > 4:
                                e.wait_ge(sm["rtc"].h, ro - 4)
                            bank = pg[(g % 2) * 4 + ti]
                            ub = ropeu[(ro // 2) % 2]
                            buf = ropet[(ro // 2) % 2]
                            e.activation(ub[:], bank[:], AF.Identity,
                                         bias=bias_t[:, oc:oc + 1]).then_inc(sm["rt"].h, 1)
                            e.wait_ge(sm["rt"].h, ro - 1)
                            e.activation(buf[0:64, :], ub[64:128, :], AF.Identity,
                                         scale=-1.0)
                            e.activation(buf[64:128, :], ub[0:64, :],
                                         AF.Identity).then_inc(sm["rt"].h, 1)

        @blk.vector
        def _(e):
            for ent in a1:
                tt = ent["tt"]
                e.wait_ge(sm["x"].h, ent["t_x"])
                e.reduce_sum(stc(ST_XSUM, tt), xbuf[tt % 2][:],
                             axis=AX.X).then_inc(sm["vs"].h, 1)
            e.wait_ge(sm["sq"].h, T_SQ_FENCE)
            e.wait_ge(sm["vs"].h, a1[-1]["t_vs"])
            stp = [a1[-1]["t_vs"]]
            def step(ins):
                stp[0] = sm["vs"].tag() if False else stp[0] + 1
                ins.then_inc(sm["vs"].h, 1)
                e.wait_ge(sm["vs"].h, stp[0])
            step(e.tensor_scalar_mul(str_(ST_MEAN, 0, NT), str_(ST_XSUM, 0, NT), 1.0 / H))
            step(e.tensor_scalar_mul(str_(ST_EX2, 0, NT), str_(ST_SSUM, 0, NT), 1.0 / H))
            step(e.tensor_mul(str_(ST_TMP, 0, NT), str_(ST_MEAN, 0, NT), str_(ST_MEAN, 0, NT)))
            step(e.tensor_sub(str_(ST_VAR, 0, NT), str_(ST_EX2, 0, NT), str_(ST_TMP, 0, NT)))
            step(e.tensor_scalar_add(str_(ST_VAR, 0, NT), str_(ST_VAR, 0, NT), EPS))
            step(e.reciprocal(str_(ST_VAR, 0, NT), str_(ST_VAR, 0, NT)))
            e.tensor_copy(str_(ST_SSUM, 32, NT), str_(ST_VAR, 0, NT)).then_inc(sm["st1"].h, 1)
            e.wait_ge(sm["st2"].h, 1)
            step(e.scalar_tensor_tensor(str_(ST_NMR, 0, NT), str_(ST_MEAN, 0, NT), -1.0,
                                        str_(ST_RSTD, 0, NT), op0=ALU.mult, op1=ALU.mult))
            e.engine_nop() if False else None
            e.tensor_copy(str_(ST_SSUM, 32, NT), str_(ST_NMR, 0, NT)).then_inc(sm["st3"].h, 1)
            if upto == "a1":
                return
            e.wait_ge(sm["set"].h, SET_SW)
            e.wait_ge(sm["seth"].h, SET_HW)
            for h2 in range(2):
                for ent in [z for z in a2 if z["h2"] == h2]:
                    lt, gi2 = ent["lt"], ent["gi2"]
                    if h2 == 1 and lt == 0 and upto != "a2":
                        h0all = [z for z in b_sweeps if z["h2"] == 0]
                        if h0all:
                            e.wait_ge(sm["qkvp"].h, h0all[-1]["t_qkvp"])
                    for gtp in range(4):
                        e.wait_ge(sm["tp"].h, ent["t_tp"][gtp])
                        dst = bass.AP(ln1T[:].tensor,
                                      ln1T[:].offset + (4 * gtp) * T + lt * 128,
                                      [ln1T[:].ap[0], [T, 4], [1, 128]])
                        e.tensor_copy(dst, tp_ps[(ent["gi2"] * 4 + gtp) % 2][:].rearrange(
                            "p (k c) -> p k c", k=4)).then_inc(sm["tpf"].h, 1)
                for ent in ([] if upto == "a2" else
                            [z for z in b_sweeps if z["h2"] == h2 and z["wi"] != 2]):
                    wi, oc, g = ent["wi"], ent["oc"], ent["g"]
                    cs, sn = (cosq_t, sinq_t) if wi == 0 else (cosk_t, sink_t)
                    bias_t = bt["bq"] if wi == 0 else bt["bk"]
                    e.wait_ge(sm["qkvp"].h, ent["t_qkvp"])
                    if g >= 2:
                        e.wait_ge(sm["sp"].h, b_sweeps[g - 2]["t_sp"])
                    for ti in range(4):
                        col0 = ti * 512
                        tcol = (h2 * T if wi == 1 else 0) + col0
                        bank = pg[(g % 2) * 4 + ti]
                        ro = ent["rt"][ti]
                        if upto == "bqm":
                            ins = e.tensor_copy(spill[g % 2][:, col0:col0 + 512], bank[:])
                            ins.then_inc(sm["evr"].h if ti == 3 else sm["rtc"].h, 1)
                            continue
                        e.wait_ge(sm["rt"].h, ro)
                        e.tensor_tensor(ropea[:], ropeu[(ro // 2) % 2][:],
                                        cs[:, tcol:tcol + 512],
                                        op=ALU.mult).then_inc(sm["rtc"].h, 1)
                        e.tensor_tensor(ropeb[:], ropet[(ro // 2) % 2][:],
                                        sn[:, tcol:tcol + 512],
                                        op=ALU.mult).then_inc(sm["rtc"].h, 1)
                        e.wait_ge(sm["rtc"].h, ent["rtc"][ti])
                        ins = e.tensor_add(spill[g % 2][:, col0:col0 + 512],
                                           ropea[:], ropeb[:])
                        if ti == 3:
                            ins.then_inc(sm["evr"].h, 1)

        @blk.tensor
        def _(e):
            if upto == "a1":
                return
            e.wait_ge(sm["set"].h, SET_SW)
            e.wait_ge(sm["seth"].h, SET_HW)
            for h2 in range(2):
                if h2 == 1 and upto != "a2":
                    h0r = [z for z in b_sweeps if z["h2"] == 0 and z["wi"] != 2]
                    h0v = [z for z in b_sweeps if z["h2"] == 0 and z["wi"] == 2]
                    if h0r:
                        e.wait_ge(sm["evr"].h, h0r[-1]["ev"][1])
                    if h0v:
                        e.wait_ge(sm["evv"].h, h0v[-1]["ev"][1])
                for ent in [z for z in a2 if z["h2"] == h2]:
                    gi2 = ent["gi2"]
                    e.wait_ge(sm["ln"].h, ent["t_ln"])
                    for gtp in range(4):
                        gi = gi2 * 4 + gtp
                        if gi >= 2:
                            pent = a2[(gi - 2) // 4]
                            e.wait_ge(sm["tpf"].h, pent["t_tpf"][(gi - 2) % 4])
                        for j in range(4):
                            kc = 4 * gtp + j
                            ins = e.transpose(tp_ps[gi % 2][:, j * 128:(j + 1) * 128],
                                              lnb[gi2 % 2][:, kc * 128:(kc + 1) * 128],
                                              ident_t[:])
                            if j == 3:
                                ins.then_inc(sm["tp"].h, 1)
                last_a2 = [z for z in a2 if z["h2"] == h2][-1]
                e.wait_ge(sm["tpf"].h, last_a2["t_tpf"][3])
                for ent in ([] if upto == "a2" else [z for z in b_sweeps if z["h2"] == h2]):
                    wi, oc, g = ent["wi"], ent["oc"], ent["g"]
                    e.wait_ge(sm["w"].h, ent["t_w"])
                    if g >= 2:
                        sem, tgt = b_sweeps[g - 2]["ev"]
                        e.wait_ge(sm[sem].h, tgt)
                    for kci in range(KC):
                        e.ldweights(wblk[g % 4][:, kci * 128:(kci + 1) * 128])
                        for ti in range(4):
                            ins = e.matmul(pg[(g % 2) * 4 + ti][:],
                                           wblk[g % 4][:, kci * 128:(kci + 1) * 128],
                                           ln1c(kci, ti * 512, 512),
                                           start=(kci == 0), stop=(kci == KC - 1))
                            if kci == KC - 1 and ti == 3:
                                ins.then_inc(sm["qkvp"].h, 1)

    if upto in ("a1", "a2", "ln1", "qkv", "bq", "bq1", "bqm", "bqk", "bv"):
        return nc

    nc.sbuf_base = sbA
    kTb = [nc.alloc_sbuf_tensor(f"kTb{i}", [128, S], BF16) for i in range(2)]
    vTb = [nc.alloc_sbuf_tensor(f"vTb{i}", [128, S], BF16) for i in range(2)]
    qTb = [nc.alloc_sbuf_tensor(f"qTb{i}", [128, T], BF16) for i in range(2)]
    vh_t = nc.alloc_sbuf_tensor("vh_t", [128, S], BF16)
    expb = [nc.alloc_sbuf_tensor(f"expb{i}", [128, 1024], BF16) for i in range(8)]
    dacc = [nc.alloc_sbuf_tensor(f"dacc{i}", [128, 1024], BF16) for i in range(4)]
    dc1 = nc.alloc_sbuf_tensor("dc1", [128, 1024], BF16)
    dc2 = nc.alloc_sbuf_tensor("dc2", [128, 1024], BF16)
    den_sb = nc.alloc_sbuf_tensor("den_sb", [128, 1024], F32)
    den2_sb = nc.alloc_sbuf_tensor("den2_sb", [128, 1024], F32)
    recip_sb = nc.alloc_sbuf_tensor("recip_sb", [128, 1024], F32)
    aev = [nc.alloc_sbuf_tensor(f"aev{i}", [128, 1024], BF16) for i in range(2)]

    nc.psum_base = psA
    scps = [nc.alloc_psum_tensor(f"scps{i}", [128, 1024], F32) for i in range(2)]
    outps = [nc.alloc_psum_tensor(f"outps{i}", [128, 1024], F32) for i in range(2)]
    nc.psum_base = psA
    denps = nc.alloc_psum_tensor("denps", [128, 1024], F32)
    nc.psum_base = psA
    tpv_ps = [nc.alloc_psum_tensor(f"tpv{i}", [128, 512], BF16) for i in range(4)]
    nc.psum_base = psA + 8

    B_QKVP = sm["qkvp"].n
    B_EVR, B_EVV, B_SP = sm["evr"].n, sm["evv"].n, sm["sp"].n

    with nc.Block("C") as blk:
        @blk.sync
        def _(e):
            e.wait_ge(sm["qkvp"].h, B_QKVP)
            e.wait_ge(sm["evr"].h, B_EVR)
            e.wait_ge(sm["evv"].h, B_EVV)
            e.wait_ge(sm["sp"].h, B_SP)
            def spill_aev(hq):
                q2 = chq(hq)
                e.wait_ge(sm["oev"].h, q2["t_oev"])
                e.dma_start(out=attn_s[q2["h"], :,
                                       q2["qb"] * 1024:(q2["qb"] + 1) * 1024],
                            in_=aev[hq % 2][:]).then_inc(sm["aev"].h, 16)
            for q in c_hq:
                if q["qb"] == 1:
                    continue
                h = q["h"]
                if h >= 2:
                    e.wait_ge(sm["hd"].h, chq((h - 2) * 2 + 1)["t_hd"])
                e.dma_start(out=kTb[h % 2][:], in_=kT_s[h, :, :]).then_inc(sm["kvq"].h, 16)
                e.dma_start(out=vTb[h % 2][:], in_=vT_s[h, :, :]).then_inc(sm["kvq"].h, 16)
                e.dma_start(out=qTb[h % 2][:], in_=qT_s[h, :, :]).then_inc(sm["kvq"].h, 16)
                if h >= 1:
                    spill_aev((h - 1) * 2)
                    spill_aev((h - 1) * 2 + 1)
            spill_aev((NHC - 1) * 2)
            spill_aev((NHC - 1) * 2 + 1)
            if upto in ("attn", "c1", "c2", "c4"):
                e.wait_ge(sm["aev"].h, sm["aev"].n)

        @blk.tensor
        def _(e):
            e.wait_ge(sm["evr"].h, B_EVR)
            e.wait_ge(sm["evv"].h, B_EVV)
            for q in c_hq:
                h, qb, hq = q["h"], q["qb"], q["hq"]
                if qb == 0:
                    e.wait_ge(sm["kvq"].h, q["t_kvq"])
                    if h > 0:
                        e.wait_ge(sm["exp"].h, chq(h * 2 - 1)["t_exp"][31])
                        e.wait_ge(sm["den"].h, chq(h * 2 - 1)["t_den"] - 1)
                    for gj in range(8):
                        if gj >= 2:
                            e.wait_ge(sm["tpvf"].h, q["t_tpvf"][gj - 2])
                        for j in range(4):
                            kc = gj * 4 + j
                            ins = e.transpose(tpv_ps[gj % 2][:, j * 128:(j + 1) * 128],
                                              vTb[h % 2][:, kc * 128:(kc + 1) * 128],
                                              ident_t[:])
                            if j == 3:
                                ins.then_inc(sm["tpv"].h, 1)
                    e.wait_ge(sm["tpvf"].h, q["t_tpvf"][7])
                def emit_scores(kc):
                    gexp = hq * 32 + kc
                    if kc == 0 and hq >= 1:
                        e.wait_ge(sm["den"].h, chq(hq - 1)["t_den"] - 1)
                    if qb == 0 and kc < 2:
                        pass
                    elif gexp >= 2:
                        pq, pkc = divmod(gexp - 2, 32)
                        e.wait_ge(sm["exp"].h, chq(pq)["t_exp"][pkc])
                    e.ldweights(kTb[h % 2][:, kc * 128:(kc + 1) * 128])
                    for half in range(2):
                        ins = e.matmul(scps[kc % 2][:, half * 512:(half + 1) * 512],
                                       kTb[h % 2][:, kc * 128:(kc + 1) * 128],
                                       qTb[h % 2][:, qb * 1024 + half * 512:
                                                   qb * 1024 + (half + 1) * 512],
                                       start=True, stop=True)
                        if half == 1:
                            ins.then_inc(sm["sc"].h, 1)

                def emit_v(kc):
                    e.wait_ge(sm["exp"].h, q["t_exp"][kc])
                    if kc == 0 and hq >= 2:
                        e.wait_ge(sm["oev"].h, chq(hq - 2)["t_oev"])
                    e.ldweights(vh_t[:, kc * 128:(kc + 1) * 128])
                    for half in range(2):
                        ins = e.matmul(outps[hq % 2][:, half * 512:(half + 1) * 512],
                                       vh_t[:, kc * 128:(kc + 1) * 128],
                                       expb[kc % 8][:, half * 512:(half + 1) * 512],
                                       start=(kc == 0), stop=(kc == 31))
                        if half == 1:
                            ins.then_inc(sm["vm"].h, 1)

                for kc in range(32):
                    emit_scores(kc)
                    if kc >= 1:
                        emit_v(kc - 1)
                emit_v(31)
                e.wait_ge(sm["dng"].h, q["t_dncmb"])
                if hq >= 1:
                    e.wait_ge(sm["den"].h, chq(hq - 1)["t_den"])
                for half in range(2):
                    ins = e.matmul(denps[:, half * 512:(half + 1) * 512],
                                   ones128_t[:],
                                   den_sb[:, half * 512:(half + 1) * 512],
                                   start=True, stop=True)
                    if half == 1:
                        ins.then_inc(sm["denb"].h, 1)
                if qb == 1:
                    e.nop().then_inc(sm["hd"].h, 1)

        @blk.scalar
        def _(e):
            e.wait_ge(sm["setc"].h, 2)
            for q in c_hq:
                for kc in range(32):
                    gexp = q["hq"] * 32 + kc
                    e.wait_ge(sm["sc"].h, q["t_sc"][kc])
                    if gexp >= 8:
                        pq, pkc = divmod(gexp - 8, 32)
                        e.wait_ge(sm["vm"].h, chq(pq)["t_vm"][pkc])
                        e.wait_ge(sm["dng"].h,
                                  chq(pq)["t_dng"][pkc // 4] - 3 + pkc % 4)
                    e.activation(expb[kc % 8][:], scps[kc % 2][:], AF.Exp,
                                 bias=consts_t[:, 0:1]).then_inc(sm["exp"].h, 1)

        @blk.vector
        def _(e):
            for q in c_hq:
                h, qb, hq = q["h"], q["qb"], q["hq"]
                if qb == 0:
                    for gj in range(8):
                        e.wait_ge(sm["tpv"].h, q["t_tpv"][gj])
                        e.tensor_copy(vh_t[:, gj * 512:(gj + 1) * 512],
                                      tpv_ps[gj % 2][:]).then_inc(sm["tpvf"].h, 1)
                for gd in range(8):
                    for bidx in range(4):
                        kc = 4 * gd + bidx
                        e.wait_ge(sm["exp"].h, q["t_exp"][kc])
                        if gd == 0:
                            ins = e.tensor_copy(dacc[bidx][:], expb[kc % 8][:])
                        else:
                            ins = e.tensor_add(dacc[bidx][:], dacc[bidx][:],
                                               expb[kc % 8][:])
                        ins.then_inc(sm["dng"].h, 1)
                e.wait_ge(sm["dng"].h, q["t_dng"][7])
                e.tensor_add(dc1[:], dacc[0][:], dacc[1][:]).then_inc(sm["dng"].h, 1)
                e.tensor_add(dc2[:], dacc[2][:], dacc[3][:]).then_inc(sm["dng"].h, 1)
                e.wait_ge(sm["dng"].h, q["t_dncmb"] - 1)
                e.tensor_add(den_sb[:], dc1[:], dc2[:]).then_inc(sm["dng"].h, 1)
                e.wait_ge(sm["denb"].h, q["t_denb"])
                e.tensor_copy(den2_sb[:], denps[:]).then_inc(sm["den"].h, 1)
                e.wait_ge(sm["den"].h, q["t_den"] - 1)
                e.reciprocal(recip_sb[:], den2_sb[:]).then_inc(sm["den"].h, 1)
                e.wait_ge(sm["den"].h, q["t_den"])
                e.wait_ge(sm["vm"].h, q["t_vm"][31])
                if hq >= 2:
                    e.wait_ge(sm["aev"].h, chq(hq - 2)["t_aev"])
                e.tensor_tensor(aev[hq % 2][:], outps[hq % 2][:], recip_sb[:],
                                op=ALU.mult).then_inc(sm["oev"].h, 1)

    if upto in ("attn", "c1", "c2", "c4"):
        return nc

    sbC = nc.sbuf_base
    WOA = 10
    woT_a = nc.alloc_sbuf_tensor("woT_a", [128, WOA * H], BF16)
    nc.sbuf_base = sbA
    woT_b = nc.alloc_sbuf_tensor("woT_b", [128, (OC - WOA) * H], BF16)
    attnD = [nc.alloc_sbuf_tensor(f"attnD{i}", [128, OC * 128], BF16) for i in range(2)]
    x2b = [nc.alloc_sbuf_tensor(f"x2b{i}", [128, H], F32) for i in range(2)]
    x2rb = [nc.alloc_sbuf_tensor(f"x2rb{i}", [128, H], F32) for i in range(2)]

    def woT_ap(cc, c0, n):
        if cc < WOA:
            return woT_a[:, cc * H + c0: cc * H + c0 + n]
        return woT_b[:, (cc - WOA) * H + c0: (cc - WOA) * H + c0 + n]

    nc.psum_base = psA
    yps = [nc.alloc_psum_tensor(f"yps{i}", [128, 512], F32) for i in range(4)]
    tp2_ps = [nc.alloc_psum_tensor(f"tp2_{i}", [128, 512], BF16) for i in range(2)]

    C_OEV, C_AEV, C_HD = sm["oev"].n, sm["aev"].n, sm["hd"].n

    with nc.Block("D") as blk:
        @blk.sync
        def _(e):
            def spill_x2(qc):
                e.wait_ge(sm["sq2"].h, d_qc[qc]["t_sq2"])
                e.dma_start(out=x2_s[qc, :, :],
                            in_=x2b[qc % 2][:]).then_inc(sm["spx2"].h, 16)
            for ent in d_qc:
                qc = ent["qc"]
                if qc >= 2:
                    e.wait_ge(sm["x2"].h, d_qc[qc - 2]["t_x2"])
                e.dma_start(out=xbuf[qc % 2][:],
                            in_=x_d[qc * 128:(qc + 1) * 128, :]).then_inc(sm["xr"].h, 16)
                if qc >= 1:
                    spill_x2(qc - 1)
            spill_x2(NQ - 1)
            for ent in d2_qc:
                qc = ent["qc"]
                e.wait_ge(sm["tpf2"].h, ent["t_tpf2"][3])
                dst = bass.AP(ln2T_s.tensor, ln2T_s.offset + qc * 128,
                              [[T, 128], [128 * T, KC], [1, 128]])
                e.dma_start(out=dst,
                            in_=spill[qc % 2][:, 0:KC * 128].rearrange(
                                "p (k c) -> p k c", k=KC)).then_inc(sm["spl2"].h, 16)

        @blk.gpsimd
        def _(e):
            e.wait_ge(sm["evr"].h, B_EVR)
            e.wait_ge(sm["evv"].h, B_EVV)
            for cc in range(OC):
                if cc == WOA:
                    e.wait_ge(sm["oev"].h, C_OEV)
                    e.wait_ge(sm["aev"].h, C_AEV)
                e.dma_start(
                    out=woT_ap(cc, 0, H).rearrange("p (o c) -> p o c", o=4),
                    in_=w_d["wot"][cc * 128:(cc + 1) * 128, :].rearrange(
                        "p (o c) -> p o c", c=512),
                ).then_inc(sm["wo"].h, 16)
            e.wait_ge(sm["oev"].h, C_OEV)
            e.wait_ge(sm["aev"].h, C_AEV)
            for ent in d_qc:
                qc = ent["qc"]
                if qc >= 2:
                    e.wait_ge(sm["y"].h, d_qc[qc - 2]["t_y"])
                src = bass.AP(attn_s.tensor, attn_s.offset + qc * 128,
                              [[T, 128], [128 * T, NH], [1, 128]])
                e.dma_start(out=attnD[qc % 2][:].rearrange("p (k c) -> p k c", k=NH),
                            in_=src).then_inc(sm["attnl"].h, 16)
            for ent in d2_qc:
                qc = ent["qc"]
                if qc >= 2:
                    e.wait_ge(sm["ln2"].h, d2_qc[qc - 2]["t_ln2"])
                e.wait_ge(sm["spx2"].h, d_qc[qc]["t_spx2"])
                e.dma_start(out=x2rb[qc % 2][:],
                            in_=x2_s[qc, :, :]).then_inc(sm["x2r"].h, 16)

        @blk.tensor
        def _(e):
            e.wait_ge(sm["oev"].h, C_OEV)
            for ent in d_qc:
                qc = ent["qc"]
                e.wait_ge(sm["attnl"].h, ent["t_attnl"])
                if qc >= 1:
                    e.wait_ge(sm["x2"].h, d_qc[qc - 1]["t_x2"])
                for cc in range(OC):
                    if qc == 0:
                        e.wait_ge(sm["wo"].h, t_wo[cc])
                    e.ldweights(attnD[qc % 2][:, cc * 128:(cc + 1) * 128])
                    for ot in range(4):
                        ins = e.matmul(yps[ot][:],
                                       attnD[qc % 2][:, cc * 128:(cc + 1) * 128],
                                       woT_ap(cc, ot * 512, 512),
                                       start=(cc == 0), stop=(cc == OC - 1))
                        if cc == OC - 1 and ot == 3:
                            ins.then_inc(sm["y"].h, 1)
            for ent in d2_qc:
                qc = ent["qc"]
                e.wait_ge(sm["ln2"].h, ent["t_ln2"])
                for g in range(4):
                    gi = qc * 4 + g
                    if gi >= 2:
                        pent = d2_qc[(gi - 2) // 4]
                        e.wait_ge(sm["tpf2"].h, pent["t_tpf2"][(gi - 2) % 4])
                    for j in range(4):
                        kc = 4 * g + j
                        ins = e.transpose(tp2_ps[gi % 2][:, j * 128:(j + 1) * 128],
                                          lnb[qc % 2][:, kc * 128:(kc + 1) * 128],
                                          ident_t[:])
                        if j == 3:
                            ins.then_inc(sm["tp2"].h, 1)

        @blk.scalar
        def _(e):
            for ent in d_qc:
                qc = ent["qc"]
                e.wait_ge(sm["x2"].h, ent["t_x2"])
                e.activation(sqtmp[:], x2b[qc % 2][:], AF.Square,
                             accum_out=stc(ST_SSUM, 32 + qc)).then_inc(sm["sq2"].h, 1)
            e.copy(str_(ST_TMP, 48, NQ), str_(ST_SSUM, 32, NQ)).then_inc(sm["sq2"].h, 1)
            e.wait_ge(sm["st21"].h, 1)
            e.activation(str_(ST_RSTD, 32, NQ), str_(ST_VAR, 32, NQ),
                         AF.Sqrt).then_inc(sm["st22"].h, 1)
            e.wait_ge(sm["st23"].h, 1)
            for ent in d2_qc:
                qc = ent["qc"]
                e.wait_ge(sm["x2r"].h, ent["t_x2r"])
                if qc >= 2:
                    e.wait_ge(sm["tp2"].h, d2_qc[qc - 2]["t_tp2"][3])
                e.activation(lnb[qc % 2][:], x2rb[qc % 2][:], AF.Identity,
                             bias=stc(ST_NMR, 32 + qc),
                             scale=stc(ST_RSTD, 32 + qc)).then_inc(sm["ln2"].h, 1)

        @blk.vector
        def _(e):
            for ent in d_qc:
                qc = ent["qc"]
                e.wait_ge(sm["y"].h, ent["t_y"])
                e.wait_ge(sm["xr"].h, ent["t_xr"])
                if qc >= 2:
                    e.wait_ge(sm["spx2"].h, d_qc[qc - 2]["t_spx2"])
                    e.wait_ge(sm["sq2"].h, d_qc[qc - 2]["t_sq2"])
                for ot in range(4):
                    ins = e.tensor_add(x2b[qc % 2][:, ot * 512:(ot + 1) * 512],
                                       yps[ot][:],
                                       xbuf[qc % 2][:, ot * 512:(ot + 1) * 512])
                    if ot == 3:
                        ins.then_inc(sm["x2"].h, 1)
                e.wait_ge(sm["x2"].h, ent["t_x2"])
                e.reduce_sum(stc(ST_XSUM, 32 + qc), x2b[qc % 2][:],
                             axis=AX.X).then_inc(sm["vs2"].h, 1)
            e.wait_ge(sm["sq2"].h, T_SQ2_FENCE)
            e.wait_ge(sm["vs2"].h, d_qc[-1]["t_vs2"])
            stp2 = [d_qc[-1]["t_vs2"]]
            def step2(ins):
                stp2[0] += 1
                ins.then_inc(sm["vs2"].h, 1)
                e.wait_ge(sm["vs2"].h, stp2[0])
            step2(e.tensor_scalar_mul(str_(ST_MEAN, 32, NQ), str_(ST_XSUM, 32, NQ), 1.0 / H))
            step2(e.tensor_scalar_mul(str_(ST_EX2, 32, NQ), str_(ST_SSUM, 32, NQ), 1.0 / H))
            step2(e.tensor_mul(str_(ST_TMP, 32, NQ), str_(ST_MEAN, 32, NQ), str_(ST_MEAN, 32, NQ)))
            step2(e.tensor_sub(str_(ST_VAR, 32, NQ), str_(ST_EX2, 32, NQ), str_(ST_TMP, 32, NQ)))
            step2(e.tensor_scalar_add(str_(ST_VAR, 32, NQ), str_(ST_VAR, 32, NQ), EPS))
            step2(e.reciprocal(str_(ST_VAR, 32, NQ), str_(ST_VAR, 32, NQ)))
            e.tensor_copy(str_(ST_SSUM, 48, NQ), str_(ST_VAR, 32, NQ)).then_inc(sm["st21"].h, 1)
            e.wait_ge(sm["st22"].h, 1)
            step2(e.scalar_tensor_tensor(str_(ST_NMR, 32, NQ), str_(ST_MEAN, 32, NQ), -1.0,
                                         str_(ST_RSTD, 32, NQ), op0=ALU.mult, op1=ALU.mult))
            e.tensor_copy(str_(ST_SSUM, 48, NQ), str_(ST_NMR, 32, NQ)).then_inc(sm["st23"].h, 1)
            for ent in d2_qc:
                qc = ent["qc"]
                for g in range(4):
                    gi = qc * 4 + g
                    e.wait_ge(sm["tp2"].h, ent["t_tp2"][g])
                    if qc >= 2 and g == 0:
                        e.wait_ge(sm["spl2"].h, d2_qc[qc - 2]["t_spl2"])
                    e.tensor_copy(spill[qc % 2][:, g * 512:(g + 1) * 512],
                                  tp2_ps[gi % 2][:]).then_inc(sm["tpf2"].h, 1)

    if upto == "wo":
        with nc.Block("Dend") as blk2:
            @blk2.sync
            def _(e):
                e.wait_ge(sm["spl2"].h, sm["spl2"].n)
                e.wait_ge(sm["spx2"].h, sm["spx2"].n)
        return nc

    nc.sbuf_base = sbA
    ln2q = [nc.alloc_sbuf_tensor(f"ln2q{i}", [128, KC * 512], BF16)
            for i in range(2)]
    h_sb = nc.alloc_sbuf_tensor("h_sb", [128, FC * 512], BF16)
    o0_sb = nc.alloc_sbuf_tensor("o0_sb", [128, OC * 512], BF16)
    mask_u8 = nc.alloc_sbuf_tensor("mask_u8", [128, 512], U8)
    mrow_sb = nc.alloc_sbuf_tensor("mrow_sb", [1, 512], BF16)
    gcp_sb = nc.alloc_sbuf_tensor("gcp_sb", [1, 512], F32)
    s1tmp = [nc.alloc_sbuf_tensor(f"s1tmp{i}", [128, 512], BF16) for i in range(2)]
    tmp0 = nc.alloc_sbuf_tensor("tmp0", [128, 512], BF16)
    finalb = [nc.alloc_sbuf_tensor(f"finalb{i}", [128, H], F32) for i in range(2)]

    nc.psum_base = psA
    hps = [nc.alloc_psum_tensor(f"hps{i}", [128, 512], F32) for i in range(2)]
    ops = [nc.alloc_psum_tensor(f"ops{i}", [128, 512], F32) for i in range(2)]
    psMid = nc.psum_base
    nc.psum_base = psA
    gps = nc.alloc_psum_tensor("gps", [1, 512], F32)
    mskps = nc.alloc_psum_tensor("mskps", [128, 512], F32)
    nc.psum_base = psMid
    psf = [nc.alloc_psum_tensor(f"psf{i}", [128, 2048], BF16) for i in range(2)]

    D_Y, D_SPL2, D_LN2 = sm["y"].n, sm["spl2"].n, sm["ln2"].n
    wE = {0: "w0at", 1: "w0bt", 2: "w1at", 3: "w1bt"}

    def eprev(bb, back):
        i = bb["stream_i"] - back
        return e_blocks[i] if i >= 0 else None

    with nc.Block("E") as blk:
        @blk.sync
        def _(e):
            e.wait_ge(sm["y"].h, D_Y)
            e.wait_ge(sm["spl2"].h, D_SPL2)
            def load_ln2q(qq):
                if qq >= 2:
                    e.wait_ge(sm["hp"].h, e_qq[qq - 2]["ph"][2]["blk"][FC - 1]["t_hp"])
                    e.wait_ge(sm["gate"].h, e_qq[qq - 2]["t_gate"])
                for kc in range(KC):
                    e.dma_start(out=ln2q[qq % 2][:, kc * 512:(kc + 1) * 512],
                                in_=ln2T_s[kc, :, qq * 512:(qq + 1) * 512]
                                ).then_inc(sm["lnq"].h, 16)
            load_ln2q(0)
            load_ln2q(1)
            for ent in e_qq:
                qq = ent["qq"]
                if qq + 2 < 4:
                    load_ln2q(qq + 2)
                for f in ent["fin"]:
                    qcB = f["qcB"]
                    e.wait_ge(sm["fin"].h, f["t_fin"])
                    e.dma_start(out=out_d[qq * 512 + qcB * 128:
                                          qq * 512 + (qcB + 1) * 128, :],
                                in_=finalb[qcB % 2][:]).then_inc(sm["outd"].h, 16)
            e.wait_ge(sm["outd"].h, sm["outd"].n)

        @blk.gpsimd
        def _(e):
            e.wait_ge(sm["y"].h, D_Y)
            e.wait_ge(sm["ln2"].h, D_LN2)
            for ent in e_qq:
                qq = ent["qq"]
                for p in ent["ph"]:
                    pi = p["pi"]
                    wd = w_d[wE[pi]]
                    nch = KC if pi in (0, 2) else FC
                    for j, bb in enumerate(p["blk"]):
                        pb = eprev(bb, 4)
                        if pb is not None:
                            e.wait_ge(sm["hp"].h if "t_hp" in pb else sm["op"].h,
                                      pb.get("t_hp") or pb.get("t_op"))
                        e.dma_start(
                            out=wblk[bb["stream_i"] % 4][:, 0:nch * 128].rearrange(
                                "p (k c) -> p k c", k=nch),
                            in_=wd[:, j * 128:(j + 1) * 128].rearrange(
                                "(k p) c -> p k c", p=128),
                        ).then_inc(sm["w2"].h, 16)
                for f in ent["fin"]:
                    qcB, gqc = f["qcB"], f["gqc"]
                    if gqc >= 2:
                        pf = e_qq[(gqc - 2) // 4]["fin"][(gqc - 2) % 4]
                        e.wait_ge(sm["fin"].h, pf["t_fin"])
                    e.dma_start(out=xbuf[qcB % 2][:],
                                in_=x2_s[gqc, :, :]).then_inc(sm["xe"].h, 16)

        @blk.tensor
        def _(e):
            e.wait_ge(sm["tpf2"].h, sm["tpf2"].n)
            for ent in e_qq:
                qq = ent["qq"]
                e.wait_ge(sm["lnq"].h, ent["t_lnq"])
                e.wait_ge(sm["setc"].h, 2)
                if qq >= 1:
                    e.wait_ge(sm["hev"].h, e_qq[qq - 1]["ph"][2]["blk"][FC - 1]["t_hev"])
                    e.wait_ge(sm["mskf"].h, e_qq[qq - 1]["t_mskf"])
                for kc in range(KC):
                    ins = e.matmul(gps[0:1, :], wgd_t[:, kc:kc + 1],
                                   ln2q[qq % 2][:, kc * 512:(kc + 1) * 512],
                                   start=(kc == 0), stop=(kc == KC - 1))
                    if kc == KC - 1:
                        ins.then_inc(sm["gate"].h, 1)
                e.wait_ge(sm["msk"].h, ent["t_msk"])
                e.matmul(mskps[:], onesrow_t[:], mrow_sb[0:1, :],
                         start=True, stop=True).then_inc(sm["mskp"].h, 1)
                for p in ent["ph"]:
                    pi = p["pi"]
                    hpass = pi in (0, 2)
                    ncha = KC if hpass else FC
                    for j, bb in enumerate(p["blk"]):
                        e.wait_ge(sm["w2"].h, bb["t_w2"])
                        fam_prev = None
                        bi = bb["stream_i"] - 1
                        cnt = 0
                        while bi >= 0:
                            cb = e_blocks[bi]
                            if (cb["pi"] in (0, 2)) == hpass:
                                cnt += 1
                                if cnt == 2:
                                    fam_prev = cb
                                    break
                            bi -= 1
                        if fam_prev is not None:
                            if "t_hev" in fam_prev:
                                e.wait_ge(sm["hev"].h, fam_prev["t_hev"])
                            elif "t_oevB" in fam_prev:
                                e.wait_ge(sm["oevB"].h, fam_prev["t_oevB"])
                            else:
                                e.wait_ge(sm["s1"].h, fam_prev["t_s1"])
                        if pi == 0 and j < 2:
                            e.wait_ge(sm["mskf"].h, ent["t_mskf"])
                        if pi in (1, 3) and j == 0:
                            e.wait_ge(sm["hev"].h, ent["ph"][pi - 1]["blk"][FC - 1]["t_hev"])
                        if pi == 2 and j == 0:
                            e.wait_ge(sm["op"].h, ent["ph"][1]["blk"][OC - 1]["t_op"])
                        if pi == 3 and j == 0 and qq >= 1:
                            pass
                        ps = (hps if hpass else ops)[j % 2]
                        rhsrc = ln2q[qq % 2] if hpass else h_sb
                        for kci in range(ncha):
                            e.ldweights(wblk[bb["stream_i"] % 4][:,
                                        kci * 128:(kci + 1) * 128])
                            ins = e.matmul(ps[:],
                                           wblk[bb["stream_i"] % 4][:,
                                           kci * 128:(kci + 1) * 128],
                                           rhsrc[:, kci * 512:(kci + 1) * 512],
                                           start=(kci == 0), stop=(kci == ncha - 1))
                            if kci == ncha - 1:
                                ins.then_inc(sm["hp"].h if hpass else sm["op"].h, 1)
                for f in ent["fin"]:
                    qcB, gqc = f["qcB"], f["gqc"]
                    if qcB == 0:
                        e.wait_ge(sm["bl"].h, ent["ph"][3]["blk"][OC - 1]["t_bl"])
                        e.wait_ge(sm["s1"].h, ent["ph"][3]["blk"][OC - 1]["t_s1"])
                    if gqc >= 2:
                        pf = e_qq[(gqc - 2) // 4]["fin"][(gqc - 2) % 4]
                        e.wait_ge(sm["fin"].h, pf["t_fin"])
                    for ocj in range(OC):
                        ins = e.transpose(psf[gqc % 2][:, ocj * 128:(ocj + 1) * 128],
                                          o0_sb[:, ocj * 512 + qcB * 128:
                                                ocj * 512 + (qcB + 1) * 128],
                                          ident_t[:])
                        if ocj == OC - 1:
                            ins.then_inc(sm["tpe"].h, 1)

        @blk.scalar
        def _(e):
            for ent in e_qq:
                for p in ent["ph"]:
                    pi = p["pi"]
                    for j, bb in enumerate(p["blk"]):
                        if pi in (0, 2):
                            e.wait_ge(sm["hp"].h, bb["t_hp"])
                            func = AF.Gelu if pi == 0 else AF.Relu
                            bias = bt["b0a"] if pi == 0 else bt["b1a"]
                            e.activation(h_sb[:, j * 512:(j + 1) * 512],
                                         hps[j % 2][:], func,
                                         bias=bias[:, j:j + 1]).then_inc(sm["hev"].h, 1)
                        elif pi == 1:
                            e.wait_ge(sm["op"].h, bb["t_op"])
                            e.activation(o0_sb[:, j * 512:(j + 1) * 512],
                                         ops[j % 2][:], AF.Identity,
                                         bias=bt["b0b"][:, j:j + 1]).then_inc(sm["oevB"].h, 1)
                        else:
                            e.wait_ge(sm["op"].h, bb["t_op"])
                            if j >= 2:
                                e.wait_ge(sm["bl"].h, p["blk"][j - 2]["t_bl"])
                            e.activation(s1tmp[j % 2][:], ops[j % 2][:], AF.Identity,
                                         bias=bt["b1b"][:, j:j + 1]).then_inc(sm["s1"].h, 1)

        @blk.vector
        def _(e):
            for ent in e_qq:
                qq = ent["qq"]
                e.wait_ge(sm["gate"].h, ent["t_gate"])
                e.tensor_copy(gcp_sb[:], gps[0:1, :]).then_inc(sm["msk"].h, 1)
                e.wait_ge(sm["msk"].h, ent["t_mskcp"])
                e.tensor_single_scalar(mrow_sb[:], gcp_sb[:], 0.0,
                                       op=ALU.is_ge).then_inc(sm["msk"].h, 1)
                e.wait_ge(sm["mskp"].h, ent["t_mskp"])
                e.tensor_copy(mask_u8[:], mskps[:]).then_inc(sm["mskf"].h, 1)
                p3 = ent["ph"][3]
                for j, bb in enumerate(p3["blk"]):
                    e.wait_ge(sm["s1"].h, bb["t_s1"])
                    e.tensor_copy(tmp0[:], o0_sb[:, j * 512:(j + 1) * 512]
                                  ).then_inc(sm["bl"].h, 1)
                    e.wait_ge(sm["bl"].h, bb["t_bl"] - 1)
                    e.select(o0_sb[:, j * 512:(j + 1) * 512], mask_u8[:],
                             tmp0[:], s1tmp[j % 2][:]).then_inc(sm["bl"].h, 1)
                for f in ent["fin"]:
                    qcB, gqc = f["qcB"], f["gqc"]
                    e.wait_ge(sm["tpe"].h, f["t_tpe"])
                    e.wait_ge(sm["xe"].h, f["t_xe"])
                    if gqc >= 2:
                        pf = e_qq[(gqc - 2) // 4]["fin"][(gqc - 2) % 4]
                        e.wait_ge(sm["outd"].h, pf["t_out"])
                    for ot in range(4):
                        ins = e.tensor_add(finalb[qcB % 2][:, ot * 512:(ot + 1) * 512],
                                           psf[gqc % 2][:, ot * 512:(ot + 1) * 512],
                                           xbuf[qcB % 2][:, ot * 512:(ot + 1) * 512])
                        if ot == 3:
                            ins.then_inc(sm["fin"].h, 1)

    return nc


_CACHE = {}


def _tables():
    inv = (1.0 / (1000000.0 ** (np.arange(0, HD, 2, dtype=np.float32) / HD))
           ).astype(np.float32)
    pos = np.arange(S, dtype=np.float32)
    fr = np.outer(pos, inv).astype(np.float32)
    cosT = np.concatenate([np.cos(fr).T, np.cos(fr).T], 0)
    sinT = np.concatenate([np.sin(fr).T, np.sin(fr).T], 0)
    return cosT, sinT


def make_inmaps(inputs):
    X = np.asarray(inputs["L0_attn_hidden_states"])
    cosT, sinT = _tables()
    sc = np.float32(1.0 / np.sqrt(HD))
    shared = {
        "wqt": np.ascontiguousarray(np.asarray(inputs["L0_attn_Wq"]).T),
        "wkt": np.ascontiguousarray(np.asarray(inputs["L0_attn_Wk"]).T),
        "wvt": np.ascontiguousarray(np.asarray(inputs["L0_attn_Wv"]).T),
        "wot": np.ascontiguousarray(np.asarray(inputs["L0_attn_Wo"]).T),
        "w0at": np.ascontiguousarray(np.asarray(inputs["L0_ffn_W0a"]).T),
        "w0bt": np.ascontiguousarray(np.asarray(inputs["L0_ffn_W0b"]).T),
        "w1at": np.ascontiguousarray(np.asarray(inputs["L0_ffn_W1a"]).T),
        "w1bt": np.ascontiguousarray(np.asarray(inputs["L0_ffn_W1b"]).T),
        "bq": np.asarray(inputs["L0_attn_bq"]),
        "bk": np.asarray(inputs["L0_attn_bk"]),
        "bv": np.asarray(inputs["L0_attn_bv"]),
        "wgt": np.ascontiguousarray(np.asarray(inputs["L0_ffn_Wg"]).T),
        "b0a": np.asarray(inputs["L0_ffn_b0a"]),
        "b0b": np.asarray(inputs["L0_ffn_b0b"]),
        "b1a": np.asarray(inputs["L0_ffn_b1a"]),
        "b1b": np.asarray(inputs["L0_ffn_b1b"]),
        "ident": np.eye(128, dtype=ml_dtypes.bfloat16),
        "ones128": np.ones((128, 128), np.float32),
        "onesrow": np.ones((1, 128), ml_dtypes.bfloat16),
    }
    in_maps = []
    for c in range(8):
        b, half = c // 2, c % 2
        roll = -half * T
        in_maps.append(dict(shared, **{
            "x": np.ascontiguousarray(np.roll(X[b], roll, axis=0)),
            "cosq": (cosT[:, half * T:(half + 1) * T] * sc).astype(ml_dtypes.bfloat16),
            "sinq": (sinT[:, half * T:(half + 1) * T] * sc).astype(ml_dtypes.bfloat16),
            "cosk": np.ascontiguousarray(np.roll(cosT, roll, axis=1)
                                         ).astype(ml_dtypes.bfloat16),
            "sink": np.ascontiguousarray(np.roll(sinT, roll, axis=1)
                                         ).astype(ml_dtypes.bfloat16),
        }))
    return in_maps


def kernel(**inputs):
    if "full" not in _CACHE:
        _CACHE["full"] = build("full")
    nc = _CACHE["full"]
    in_maps = make_inmaps(inputs)
    res = run_bass_kernel_spmd(nc, in_maps, core_ids=list(range(8)))
    OUT = np.empty((B, S, H), np.float32)
    for c in range(8):
        b, half = c // 2, c % 2
        OUT[b, half * T:(half + 1) * T, :] = res.results[c]["out"]
    return OUT
